# version 08
# baseline (speedup 1.0000x reference)
import contextlib
import numpy as np
import concourse.bass as bass
import concourse.mybir as mybir
from concourse.bass_utils import run_bass_kernel_spmd

F32 = mybir.dt.float32
BF16 = mybir.dt.bfloat16
AF = mybir.ActivationFunctionType
ALU = mybir.AluOpType
AX = mybir.AxisListType.X

S = 2048
D = 1024
DC = 8
NH = 8
FI = 4096
NE = 8
NU = NE + 1
NTT = 16
NT = 512
TBN = 4
NIG = 8
ICG = 4
EPS_LN = 1e-5
EPS_RT = 1e-9

ENGS = ("tensor", "vector", "scalar", "gpsimd", "sync")
DMA_SEMS = {"ds", "dc", "dg", "ds0", "ds1", "dc0", "dc1", "dgb"}
ALL_SEMS = ("pe", "act", "dve", "pool", "ds", "dc", "dg",
            "ds0", "ds1", "dc0", "dc1", "dgb")


class Plan:

    def __init__(self):
        self.ops = {e: [] for e in ENGS}
        self.cnt = {}
        self.waits = []
        self.last_wait = {}

    def op(self, eng, meth, *args, inc=None, **kwargs):
        step = 16 if inc in DMA_SEMS else 1
        if inc is not None:
            self.cnt[inc] = self.cnt.get(inc, 0) + step
        v = self.cnt.get(inc) if inc is not None else None

        def f(h, sems, meth=meth, args=args, kwargs=kwargs, inc=inc, step=step):
            ins = getattr(h, meth)(*args, **kwargs)
            if inc is not None:
                ins.then_inc(sems[inc], step)
        self.ops[eng].append(f)
        return v

    def wait(self, eng, sem, val):
        if val is None or val <= 0:
            return
        if self.last_wait.get((eng, sem), -1) >= val:
            return
        self.last_wait[(eng, sem)] = val
        self.waits.append((sem, val))

        def f(h, sems, sem=sem, val=val):
            h.wait_ge(sems[sem], val)
        self.ops[eng].append(f)

    def val(self, sem):
        return self.cnt.get(sem, 0)

    def check(self):
        for sem, val in self.waits:
            assert val <= self.cnt.get(sem, 0), \
                f"wait {sem}>={val} never satisfied (final {self.cnt.get(sem, 0)})"


def build_nc(phases=3, debug=()):
    nc = bass.Bass("TRN2", target_bir_lowering=False)
    debug = set(debug)

    ap = lambda t: t.ap()
    x_d = ap(nc.dram_tensor("x", [S, D], F32, kind="ExternalInput"))
    wq_d = ap(nc.dram_tensor("wq", [NH, D, 128], F32, kind="ExternalInput"))
    wk_d = ap(nc.dram_tensor("wk", [D, 128], F32, kind="ExternalInput"))
    wv_d = ap(nc.dram_tensor("wv", [D, 128], F32, kind="ExternalInput"))
    wot_d = ap(nc.dram_tensor("wot", [NH, 128, D], F32, kind="ExternalInput"))
    gw_d = ap(nc.dram_tensor("gw", [D, NE], F32, kind="ExternalInput"))
    gb_d = ap(nc.dram_tensor("gb", [NE], F32, kind="ExternalInput"))
    w1_d = [ap(nc.dram_tensor("shw1", [D, FI], F32, kind="ExternalInput"))]
    b1_d = [ap(nc.dram_tensor("shb1", [FI], F32, kind="ExternalInput"))]
    w2_d = [ap(nc.dram_tensor("shw2", [FI, D], F32, kind="ExternalInput"))]
    b2_d = [ap(nc.dram_tensor("shb2", [D], F32, kind="ExternalInput"))]
    ew1_d = ap(nc.dram_tensor("ew1", [NE, D, FI], F32, kind="ExternalInput"))
    eb1_d = ap(nc.dram_tensor("eb1", [NE, FI], F32, kind="ExternalInput"))
    ew2_d = ap(nc.dram_tensor("ew2", [NE, FI, D], F32, kind="ExternalInput"))
    eb2_d = ap(nc.dram_tensor("eb2", [NE, D], F32, kind="ExternalInput"))
    for e in range(NE):
        w1_d.append(ew1_d[e])
        b1_d.append(eb1_d[e])
        w2_d.append(ew2_d[e])
        b2_d.append(eb2_d[e])
    out_d = ap(nc.dram_tensor("out", [S, D], F32, kind="ExternalOutput"))
    dbg_d = {}
    if debug & {"resA", "ffn", "g"}:
        dbg_d["res"] = ap(nc.dram_tensor("dbg_res", [128, DC, S], F32, kind="ExternalOutput"))
    if "router" in debug:
        dbg_d["wet"] = ap(nc.dram_tensor("dbg_wet", [NE, S], F32, kind="ExternalOutput"))
    if "router2" in debug:
        dbg_d["gbp"] = ap(nc.dram_tensor("dbg_gbp", [8, 1], F32, kind="ExternalOutput"))
        dbg_d["grow"] = ap(nc.dram_tensor("dbg_grow", [1, 8], F32, kind="ExternalOutput"))
        dbg_d["gsb"] = ap(nc.dram_tensor("dbg_gsb", [8, S], F32, kind="ExternalOutput"))
        dbg_d["rtok"] = ap(nc.dram_tensor("dbg_rtok", [128, 128], F32, kind="ExternalOutput"))
        dbg_d["wtok"] = ap(nc.dram_tensor("dbg_wtok", [128, 128], F32, kind="ExternalOutput"))
    if "ocat" in debug:
        dbg_d["ocat"] = ap(nc.dram_tensor("dbg_ocat", [128, NH * S], BF16, kind="ExternalOutput"))
    if "kv" in debug:
        dbg_d["kv"] = ap(nc.dram_tensor("dbg_kv", [128, 2, S], BF16, kind="ExternalOutput"))

    p = Plan()

    with contextlib.ExitStack() as ctx:
        E = ctx.enter_context
        result = E(nc.sbuf_tensor("result", [128, DC, S], F32))
        xn_fm = E(nc.sbuf_tensor("xn_fm", [128, DC, S], BF16))
        abf = E(nc.sbuf_tensor("abf", [128, 27648], BF16))
        af = E(nc.sbuf_tensor("af", [128, 13312], F32))
        ident_f = E(nc.sbuf_tensor("ident_f", [128, 128], F32))
        ident_b = E(nc.sbuf_tensor("ident_b", [128, 128], BF16))
        ones_col = E(nc.sbuf_tensor("ones_col", [128, 1], BF16))
        ones_row = E(nc.sbuf_tensor("ones_row", [1, 128], F32))
        ebias1 = E(nc.sbuf_tensor("ebias1", [128, NU * 32], F32))
        ebias2 = E(nc.sbuf_tensor("ebias2", [128, NU * 8], F32))
        gatebp = E(nc.sbuf_tensor("gatebp", [128, 1], F32))
        gate_st = E(nc.sbuf_tensor("gate_st", [128, 8, 8], F32))
        rtk = E(nc.sbuf_tensor("rtk", [128, 16], F32))

        bf = abf[:]
        fa = af[:]
        xtokA = fa[:, 0:2048].rearrange("p (b n) -> p b n", b=2)
        xntokA = fa[:, 2048:4096].rearrange("p (b n) -> p b n", b=2)
        sqA = fa[:, 4096:6144].rearrange("p (b n) -> p b n", b=2)
        colsA = fa[:, 6144:6160]
        xnbfA = bf[:, 0:2048].rearrange("p (b n) -> p b n", b=2)
        wstage = fa[:, 0:4096].rearrange("p (b n) -> p b n", b=2)
        wrowstage = fa[0:1, 0:4096]
        wbc = fa[:, 4096:8192].rearrange("p (b n) -> p b n", b=2)
        g1 = fa[:, 8192:9216].rearrange("p (b n) -> p b n", b=2)
        rtok = fa[:, 8192:8320]
        wtok = fa[:, 8320:8448]
        wrow = fa[0:1, 9216:11264]
        wet = fa[0:8, 11264:13312]
        w1bf = bf[:, 0:8192].rearrange("p (b c i) -> p b c i", b=2, c=DC)
        w2bf = bf[:, 8192:16384].rearrange("p (b j d) -> p b j d", b=2, j=ICG)
        h_sb = bf[:, 16384:24576].rearrange("p (j n) -> p j n", j=ICG)
        ocat = bf[:, 0:16384].rearrange("p (h n) -> p h n", h=NH)
        k_sb = bf[:, 16384:18432]
        vt_sb = bf[:, 18432:20480]
        v_sb = bf[:, 20480:22528]
        q_sb = bf[:, 20480:24576].rearrange("p (b n) -> p b n", b=2)
        e_sb = bf[:, 24576:25600].rearrange("p (b n) -> p b n", b=2)
        wqkv_bf = bf[:, 25600:27648].rearrange("p (b c k) -> p b c k", b=2, c=DC)
        wo_bf = bf[:, 16384:24576].rearrange("p (h d) -> p h d", h=NH)
        rb_sb = fa[:, 0:1024].rearrange("p (b n) -> p b n", b=2)
        gt_sb = fa[:, 1024:3072].rearrange("p (b n) -> p b n", b=2)
        xtokC = fa[:, 3072:5120].rearrange("p (b n) -> p b n", b=2)
        wqkv_st = fa[:, 5120:7168].rearrange("p (b c k) -> p b c k", b=2, c=DC)
        wo_st = fa[:, 7168:9216].rearrange("p (b n) -> p b n", b=2)
        sumrec = fa[0:1, 9216:9728]

        psa = E(nc.psum_tensor("psa", [128, 7, 512], F32))
        psx = E(nc.psum_tensor("psx", [128, 1024], BF16))

        sems = {}
        for s_ in ALL_SEMS:
            sems[s_] = E(nc.semaphore(s_))

        guards = {}

        def guard_wait(eng, *keys):
            for key in keys:
                g = guards.get(key)
                if g:
                    p.wait(eng, g[0], g[1])

        def barrier():
            vals = [(s_, p.val(s_)) for s_ in ALL_SEMS]
            for e_ in ENGS:
                for s_, v_ in vals:
                    p.wait(e_, s_, v_)

        def pe_flush():
            p.op("tensor", "matmul", psa[0:128, 6, 256:384], ident_b[:],
                 ident_b[:], start=True, stop=True)
            p.op("tensor", "matmul", psa[0:128, 6, 384:512], ident_b[:],
                 ident_b[:], start=True, stop=True, inc="pe")
            return p.val("pe")

        p.op("gpsimd", "memset", ident_f[:], 0.0)
        p.op("gpsimd", "memset", ident_b[:], 0.0, inc="pool")
        p.wait("gpsimd", "pool", p.val("pool"))
        p.op("gpsimd", "affine_select", out=ident_f[:], in_=ident_f[:],
             compare_op=ALU.not_equal, fill=1.0, base=0, pattern=[[-1, 128]], channel_multiplier=1)
        p.op("gpsimd", "affine_select", out=ident_b[:], in_=ident_b[:],
             compare_op=ALU.not_equal, fill=1.0, base=0, pattern=[[-1, 128]], channel_multiplier=1)
        p.op("gpsimd", "memset", ones_col[:], 1.0)
        p.op("gpsimd", "memset", ones_row[:], 1.0, inc="pool")
        v_const = p.val("pool")
        p.wait("tensor", "pool", v_const)

        for u in range(NU):
            p.op("gpsimd", "dma_start", out=ebias1[:, u * 32:(u + 1) * 32],
                 in_=b1_d[u].rearrange("(c p) -> p c", p=128),
                 allow_slow_non_contiguous=True, inc="dgb")
            p.op("gpsimd", "dma_start", out=ebias2[:, u * 8:(u + 1) * 8],
                 in_=b2_d[u].rearrange("(c p) -> p c", p=128),
                 allow_slow_non_contiguous=True, inc="dgb")
        p.op("gpsimd", "dma_start", out=gatebp[0:8, :],
             in_=gb_d.rearrange("(p a) -> p a", a=1),
             allow_slow_non_contiguous=True, inc="dgb")
        v_bias_dma = p.val("dgb")

        x_rows = x_d.rearrange("(t p) d -> t p d", p=128)
        stat = fa[:, 6144:6208]
        for tt in range(NTT):
            b = tt % 2
            guard_wait("sync", ("xtokA", b))
            dv = p.op("sync", "dma_start", out=xtokA[:, b, :], in_=x_rows[tt], inc=f"dc{b}")
            p.wait("vector", f"dc{b}", dv)
            p.op("vector", "reduce_sum", stat[:, tt:tt + 1], xtokA[:, b, :], axis=AX, inc="dve")
            p.wait("scalar", f"dc{b}", dv)
            guard_wait("scalar", ("sqA", b))
            p.op("scalar", "activation", sqA[:, b, :], xtokA[:, b, :], AF.Square, inc="act")
            p.wait("vector", "act", p.val("act"))
            p.op("vector", "reduce_sum", stat[:, 16 + tt:17 + tt], sqA[:, b, :], axis=AX, inc="dve")
            guards[("xtokA", b)] = ("dve", p.val("dve"))
            guards[("sqA", b)] = ("dve", p.val("dve"))
        s1 = stat[:, 0:16]; s2 = stat[:, 16:32]; rstd = stat[:, 32:48]; nmr = stat[:, 48:64]

        def vfence():
            p.wait("vector", "dve", p.val("dve"))
        p.wait("vector", "dve", p.val("dve"))
        p.op("vector", "tensor_scalar", s1, s1, 1.0 / D, None, ALU.mult)
        p.op("vector", "tensor_scalar", s2, s2, 1.0 / D, None, ALU.mult, inc="dve")
        vfence()
        p.op("vector", "tensor_mul", rstd, s1, s1, inc="dve")
        vfence()
        p.op("vector", "tensor_sub", s2, s2, rstd, inc="dve")
        vfence()
        p.op("vector", "tensor_scalar", s2, s2, EPS_LN, None, ALU.add, inc="dve")
        p.wait("scalar", "dve", p.val("dve"))
        p.op("scalar", "activation", s2, s2, AF.Sqrt, inc="act")
        p.wait("vector", "act", p.val("act"))
        p.op("vector", "reciprocal", rstd, s2, inc="dve")
        vfence()
        p.op("vector", "scalar_tensor_tensor", nmr, s1, -1.0, rstd, ALU.mult, ALU.mult, inc="dve")
        vfence()

        xtok4 = [xtokA[:, 0, :], xtokA[:, 1, :], sqA[:, 0, :], sqA[:, 1, :]]
        for b4 in range(4):
            guards[("xtok4", b4)] = ("dve", p.val("dve"))
        for tt in range(NTT):
            b = tt % 4
            x4 = xtok4[b]
            guard_wait("sync", ("xtok4", b))
            dv = p.op("sync", "dma_start", out=x4, in_=x_rows[tt], inc=f"dc{b % 2}")
            p.wait("vector", f"dc{b % 2}", dv)
            guard_wait("vector", ("xntokA", tt % 2))
            p.op("vector", "tensor_scalar", xntokA[:, tt % 2, :], x4,
                 rstd[:, tt:tt + 1], nmr[:, tt:tt + 1], ALU.mult, ALU.add, inc="dve")
            v_xn = p.val("dve")
            guards[("xtok4", b)] = ("dve", v_xn)
            p.wait("scalar", "dve", v_xn)
            guard_wait("scalar", ("xnbfA", tt % 2))
            p.op("scalar", "activation", xnbfA[:, tt % 2, :], xntokA[:, tt % 2, :], AF.Copy, inc="act")
            v_xnbf = p.val("act")
            p.wait("tensor", "act", v_xnbf)
            guard_wait("tensor", "psx")
            for c in range(DC):
                p.op("tensor", "transpose", psx[:, c * 128:(c + 1) * 128],
                     xnbfA[:, tt % 2, c * 128:(c + 1) * 128], ident_b[:])
            v_trb = pe_flush()
            p.wait("tensor", "dve", v_xn)
            for half in range(2):
                guard_wait("tensor", ("psa", half))
                for c4 in range(4):
                    c = half * 4 + c4
                    p.op("tensor", "transpose", psa[:, half, c4 * 128:(c4 + 1) * 128],
                         xntokA[:, tt % 2, c * 128:(c + 1) * 128], ident_f[:])
            v_trf = pe_flush()
            p.wait("vector", "pe", v_trb)
            p.op("vector", "tensor_copy",
                 xn_fm[:, :, tt * 128:(tt + 1) * 128],
                 psx[:].rearrange("p (c n) -> p c n", c=DC), inc="dve")
            guards["psx"] = ("dve", p.val("dve"))
            guards[("xnbfA", tt % 2)] = ("pe", v_trb)
            p.wait("vector", "pe", v_trf)
            for half in range(2):
                p.op("vector", "tensor_copy",
                     result[:, half * 4:(half + 1) * 4, tt * 128:(tt + 1) * 128],
                     psa[:, half, :].rearrange("p (c n) -> p c n", c=4), inc="dve")
                guards[("psa", half)] = ("dve", p.val("dve"))
            guards[("xntokA", tt % 2)] = ("pe", v_trf)
        vA_dve = p.val("dve")

        barrier()
        p.wait("vector", "dgb", v_bias_dma)
        p.wait("scalar", "dgb", v_bias_dma)

        if "resA" in debug:
            p.op("gpsimd", "dma_start", out=dbg_d["res"], in_=result[:], inc="dg")

        dvgw = p.op("sync", "dma_start", out=gate_st[:],
                    in_=gw_d.rearrange("(c p) e -> p c e", p=128), inc="dc")
        gate_sb = fa[0:8, 9216:11264]
        def router_p1():
            p.wait("tensor", "dc", dvgw)
            for tb in range(TBN):
                bank = tb % 2
                guard_wait("tensor", ("psa", bank))
                for c in range(DC):
                    p.op("tensor", "matmul", psa[0:8, bank, :], gate_st[:, c, :],
                         result[:, c, tb * NT:(tb + 1) * NT],
                         start=(c == 0), stop=(c == DC - 1),
                         inc="pe" if c == DC - 1 else None)
                v = p.val("pe")
                p.wait("vector", "pe", v)
                guard_wait("vector", "wrowstage")
                p.op("vector", "tensor_scalar", gate_sb[:, tb * NT:(tb + 1) * NT],
                     psa[0:8, bank, :], gatebp[0:8, :], None, ALU.add, inc="dve")
                guards[("psa", bank)] = ("dve", p.val("dve"))

        def router_p2():
            v_gate = p.val("dve")
            if "router2" in debug:
                p.wait("gpsimd", "dve", v_gate)
                p.op("gpsimd", "dma_start", out=dbg_d["gbp"], in_=gatebp[0:8, :], inc="dg")
                p.op("gpsimd", "dma_start", out=dbg_d["gsb"], in_=gate_sb, inc="dg")
            p.wait("tensor", "dve", v_gate)
            for tt in range(NTT):
                bank = 2 + (tt % 2)
                guard_wait("tensor", ("psa", bank))
                p.op("tensor", "transpose", psa[0:128, bank, 0:8],
                     gate_sb[:, tt * 128:(tt + 1) * 128], ident_f[0:8, 0:8])
                v = pe_flush()
                p.wait("vector", "pe", v)
                p.op("vector", "tensor_copy", rtok[:, tt * 8:(tt + 1) * 8], psa[:, bank, 0:8], inc="dve")
                guards[("psa", bank)] = ("dve", p.val("dve"))
            for tt in range(NTT):
                r8 = rtok[:, tt * 8:(tt + 1) * 8]
                w8 = wtok[:, tt * 8:(tt + 1) * 8]
                c = rtk
                def vf():
                    p.wait("vector", "dve", p.val("dve"))
                p.op("vector", "reduce_max", c[:, 0:1], r8, axis=AX, negate=True, inc="dve")
                p.wait("scalar", "dve", p.val("dve"))
                p.op("scalar", "activation", w8, r8, AF.Exp, bias=c[:, 0:1], inc="act")
                p.wait("vector", "act", p.val("act"))
                p.op("vector", "reduce_sum", c[:, 1:2], w8, axis=AX, inc="dve")
                vf()
                p.op("vector", "reciprocal", c[:, 2:3], c[:, 1:2], inc="dve")
                vf()
                p.op("vector", "tensor_scalar", r8, w8, c[:, 2:3], None, ALU.mult, inc="dve")
                vf()
                p.op("vector", "reduce_max", c[:, 3:4], r8, axis=AX, inc="dve")
                vf()
                p.op("vector", "tensor_scalar", c[:, 8:16], r8, c[:, 3:4], None, ALU.is_equal, inc="dve")
                vf()
                p.op("vector", "scalar_tensor_tensor", w8, c[:, 8:16], -2.0, r8, ALU.mult, ALU.add, inc="dve")
                vf()
                p.op("vector", "reduce_max", c[:, 4:5], w8, axis=AX, inc="dve")
                vf()
                p.op("vector", "tensor_scalar", w8, r8, c[:, 4:5], None, ALU.is_ge, inc="dve")
                p.op("vector", "tensor_add", c[:, 5:6], c[:, 3:4], c[:, 4:5], inc="dve")
                vf()
                p.op("vector", "tensor_scalar", c[:, 5:6], c[:, 5:6], EPS_RT, None, ALU.add, inc="dve")
                vf()
                p.op("vector", "reciprocal", c[:, 6:7], c[:, 5:6], inc="dve")
                p.op("vector", "tensor_mul", w8, w8, r8, inc="dve")
                vf()
                p.op("vector", "tensor_scalar", w8, w8, c[:, 6:7], None, ALU.mult, inc="dve")

        def router_p3():
            v_wtok = p.val("dve")
            if "router2" in debug:
                p.wait("gpsimd", "dve", v_wtok)
                p.op("gpsimd", "dma_start", out=dbg_d["rtok"], in_=rtok, inc="dg")
                p.op("gpsimd", "dma_start", out=dbg_d["wtok"], in_=wtok, inc="dg")
            p.wait("tensor", "dve", v_wtok)
            for tt in range(NTT):
                bank = 2 + (tt % 2)
                guard_wait("tensor", ("psa", bank))
                p.op("tensor", "transpose", psa[0:8, bank, 0:128],
                     wtok[:, tt * 8:(tt + 1) * 8], ident_f[:])
                v = pe_flush()
                p.wait("vector", "pe", v)
                p.op("vector", "tensor_copy", wet[:, tt * 128:(tt + 1) * 128],
                     psa[0:8, bank, 0:128], inc="dve")
                guards[("psa", bank)] = ("dve", p.val("dve"))
            v_wet = p.val("dve")
            if "router" in debug:
                p.wait("gpsimd", "dve", v_wet)
                p.op("gpsimd", "dma_start", out=dbg_d["wet"], in_=wet, inc="dg")
            return v_wet

        if phases >= 2:
            barrier()

            wbc_pool_guard = {}

            wrow_dv = {}

            def wbc_dma(e):
                guard_wait("gpsimd", "wrow")
                p.wait("gpsimd", "dve", v_wet_h[0])
                wrow_dv[e] = p.op("gpsimd", "dma_start", out=wrow, in_=wet[e:e + 1, :], inc="dg")

            def build_wbc(e):
                if e not in wrow_dv:
                    wbc_dma(e)
                dv = wrow_dv[e]
                p.wait("tensor", "dg", dv)
                pg = wbc_pool_guard.get(e % 2)
                if pg:
                    p.wait("vector", "pool", pg)
                for tb in range(TBN):
                    guard_wait("tensor", ("psa", 6))
                    p.op("tensor", "matmul", psa[:, 6, :], ones_row[:],
                         wrow[:, tb * NT:(tb + 1) * NT], start=True, stop=True, inc="pe")
                    v = p.val("pe")
                    guards["wrow"] = ("pe", v)
                    p.wait("vector", "pe", v)
                    guard_wait("vector", ("wbc", e % 2))
                    p.op("vector", "tensor_copy", wbc[:, e % 2, tb * NT:(tb + 1) * NT],
                         psa[:, 6, :], inc="dve")
                    guards[("psa", 6)] = ("dve", p.val("dve"))
                return p.val("dve")

            wstage_slot = [0]

            def dma_w1(u, ig):
                dvs = []
                for half in range(2):
                    slot = wstage_slot[0]; wstage_slot[0] ^= 1
                    guard_wait("sync", ("wstage", slot))
                    src = w1_d[u].rearrange("(c p) i -> p c i", p=128)[
                        :, :, ig * 512 + half * 256: ig * 512 + (half + 1) * 256]
                    dv = p.op("sync", "dma_start",
                              out=wstage[:, slot, :].rearrange("p (c i) -> p c i", c=DC),
                              in_=src, inc=f"ds{slot}")
                    dvs.append((slot, dv))
                return dvs

            def conv_w1(ig, dvs):
                buf = ig % 2
                guard_wait("scalar", ("w1bf", buf))
                for half, (slot, dv) in enumerate(dvs):
                    p.wait("scalar", f"ds{slot}", dv)
                    p.op("scalar", "activation",
                         w1bf[:, buf, :, half * 256:(half + 1) * 256],
                         wstage[:, slot, :].rearrange("p (c i) -> p c i", c=DC), AF.Copy, inc="act")
                    guards[("wstage", slot)] = ("act", p.val("act"))
                return p.val("act")

            def dma_w2(u, ig):
                dvs = []
                for half in range(2):
                    slot = wstage_slot[0]; wstage_slot[0] ^= 1
                    guard_wait("sync", ("wstage", slot))
                    src = w2_d[u].rearrange("(g p) d -> p g d", p=128)[
                        :, ig * ICG + half * 2: ig * ICG + (half + 1) * 2, :]
                    dv = p.op("sync", "dma_start",
                              out=wstage[:, slot, :].rearrange("p (g d) -> p g d", g=2),
                              in_=src, inc=f"ds{slot}")
                    dvs.append((slot, dv))
                return dvs

            def conv_w2(ig, dvs):
                buf = ig % 2
                guard_wait("scalar", ("w2bf", buf))
                for half, (slot, dv) in enumerate(dvs):
                    p.wait("scalar", f"ds{slot}", dv)
                    p.op("scalar", "activation",
                         w2bf[:, buf, half * 2:(half + 1) * 2, :],
                         wstage[:, slot, :].rearrange("p (j d) -> p j d", j=2), AF.Copy, inc="act")
                    guards[("wstage", slot)] = ("act", p.val("act"))
                return p.val("act")

            wbc_ready = {}
            v_wet_h = [None]
            d1 = dma_w1(0, 0)
            w1_ready = conv_w1(0, d1)
            d2 = dma_w2(0, 0)

            gelu_j = [0]
            units = [(u, ig) for u in range(NU) for ig in range(NIG)]
            for idx, (u, ig) in enumerate(units):
                is_exp = u >= 1
                e = u - 1
                buf = ig % 2
                if is_exp and ig == 0:
                    p.wait("gpsimd", "dve", wbc_ready[e])
                w2_ready = conv_w2(ig, d2)
                nxt = units[idx + 1] if idx + 1 < len(units) else None
                if nxt:
                    d1 = dma_w1(*nxt)
                p.wait("tensor", "act", w1_ready)
                h_done = {}
                for ic in range(ICG):
                    icg = ig * ICG + ic
                    for tb in range(TBN):
                        bank = (ic * TBN + tb) % 3
                        guard_wait("tensor", ("psa", bank))
                        for c in range(DC):
                            p.op("tensor", "matmul", psa[:, bank, :],
                                 w1bf[:, buf, c, ic * 128:(ic + 1) * 128],
                                 xn_fm[:, c, tb * NT:(tb + 1) * NT],
                                 start=(c == 0), stop=(c == DC - 1),
                                 inc="pe" if c == DC - 1 else None)
                        v_h = p.val("pe")
                        guards[("w1bf", buf)] = ("pe", v_h)
                        p.wait("scalar", "pe", v_h)
                        bias_ap = ebias1[:, u * 32 + icg:u * 32 + icg + 1]
                        if is_exp:
                            j = gelu_j[0] % 2
                            gelu_j[0] += 1
                            guard_wait("scalar", ("g1", j))
                            p.op("scalar", "activation", g1[:, j, :], psa[:, bank, :],
                                 AF.Gelu_apprx_tanh, bias=bias_ap, inc="act")
                            v_gl = p.val("act")
                            guards[("psa", bank)] = ("act", v_gl)
                            p.wait("gpsimd", "act", v_gl)
                            guard_wait("gpsimd", "h_sb")
                            p.op("gpsimd", "tensor_mul", h_sb[:, ic, tb * NT:(tb + 1) * NT],
                                 g1[:, j, :], wbc[:, e % 2, tb * NT:(tb + 1) * NT], inc="pool")
                            guards[("g1", j)] = ("pool", p.val("pool"))
                            h_done[(ic, tb)] = ("pool", p.val("pool"))
                        else:
                            guard_wait("scalar", "h_sb")
                            p.op("scalar", "activation", h_sb[:, ic, tb * NT:(tb + 1) * NT],
                                 psa[:, bank, :], AF.Gelu_apprx_tanh, bias=bias_ap, inc="act")
                            guards[("psa", bank)] = ("act", p.val("act"))
                            h_done[(ic, tb)] = ("act", p.val("act"))
                if is_exp:
                    wbc_pool_guard[e % 2] = p.val("pool")
                if u == 0:
                    if ig == 0:
                        router_p1()
                    elif ig == 1:
                        router_p2()
                    elif ig == 2:
                        v_wet_h[0] = router_p3()
                if nxt:
                    w1_ready = conv_w1(nxt[1], d1)
                    d2 = dma_w2(*nxt)
                if u + 1 < NU and ig == (2 if u == 0 else 6):
                    pass
                if u + 1 < NU and ig == (3 if u == 0 else 0):
                    wbc_ready[u] = build_wbc(u)
                if u + 1 < NU and ig == (2 if u == 0 else 5):
                    wbc_dma(u)
                p.wait("tensor", "act", w2_ready)
                for dcc in range(DC):
                    for tb in range(TBN):
                        bank = 3 + (dcc * TBN + tb) % 3
                        guard_wait("tensor", ("psa", bank))
                        sem_h, v_hh = h_done[(ICG - 1, tb)]
                        p.wait("tensor", sem_h, v_hh)
                        for j in range(ICG):
                            p.op("tensor", "matmul", psa[:, bank, :],
                                 w2bf[:, buf, j, dcc * 128:(dcc + 1) * 128],
                                 h_sb[:, j, tb * NT:(tb + 1) * NT],
                                 start=(j == 0), stop=(j == ICG - 1),
                                 inc="pe" if j == ICG - 1 else None)
                        v_o = p.val("pe")
                        guards[("w2bf", buf)] = ("pe", v_o)
                        p.wait("vector", "pe", v_o)
                        p.op("vector", "tensor_tensor",
                             result[:, dcc, tb * NT:(tb + 1) * NT],
                             result[:, dcc, tb * NT:(tb + 1) * NT],
                             psa[:, bank, :], ALU.add, inc="dve")
                        guards[("psa", bank)] = ("dve", p.val("dve"))
                guards["h_sb"] = ("pe", p.val("pe"))
                if is_exp:
                    p.op("vector", "scalar_tensor_tensor",
                         result[:, ig, :], wbc[:, e % 2, :],
                         ebias2[:, u * 8 + ig:u * 8 + ig + 1],
                         result[:, ig, :], ALU.mult, ALU.add, inc="dve")
                    if ig == NIG - 1:
                        guards[("wbc", e % 2)] = ("dve", p.val("dve"))
                else:
                    p.op("vector", "tensor_scalar", result[:, ig, :], result[:, ig, :],
                         ebias2[:, u * 8 + ig:u * 8 + ig + 1], None, ALU.add, inc="dve")
            v_ffn = p.val("dve")
            if "ffn" in debug:
                p.wait("gpsimd", "dve", v_ffn)
                p.op("gpsimd", "dma_start", out=dbg_d["res"], in_=result[:], inc="dg")

        if phases >= 3:
            barrier()

            for dcc in range(DC):
                p.op("scalar", "activation", result[:, dcc, :], result[:, dcc, :],
                     AF.Silu, inc="act")
            v_g = p.val("act")
            if "g" in debug:
                p.wait("gpsimd", "act", v_g)
                p.op("gpsimd", "dma_start", out=dbg_d["res"], in_=result[:], inc="dg")

            def stream_qkv(src_ap, slot):
                guard_wait("sync", ("wqkv_st", slot))
                dv = p.op("sync", "dma_start", out=wqkv_st[:, slot, :, :],
                          in_=src_ap.rearrange("(c p) k -> p c k", p=128), inc=f"ds{slot}")
                p.wait("scalar", f"ds{slot}", dv)
                guard_wait("scalar", ("wqkv_bf", slot))
                p.op("scalar", "activation",
                     wqkv_bf[:, slot, :, :].rearrange("p c k -> p (c k)"),
                     wqkv_st[:, slot, :, :].rearrange("p c k -> p (c k)"), AF.Copy, inc="act")
                guards[("wqkv_st", slot)] = ("act", p.val("act"))
                return p.val("act")

            def proj_2048(slot, dst, vready, banks=(0, 1)):
                p.wait("tensor", "act", vready)
                for nb in range(TBN):
                    bank = banks[nb % len(banks)]
                    guard_wait("tensor", ("psa", bank))
                    for c in range(DC):
                        p.op("tensor", "matmul", psa[:, bank, :],
                             wqkv_bf[:, slot, c, :],
                             xn_fm[:, c, nb * NT:(nb + 1) * NT],
                             start=(c == 0), stop=(c == DC - 1),
                             inc="pe" if c == DC - 1 else None)
                    v = p.val("pe")
                    guards[("wqkv_bf", slot)] = ("pe", v)
                    p.wait("vector", "pe", v)
                    p.op("vector", "tensor_copy", dst[:, nb * NT:(nb + 1) * NT],
                         psa[:, bank, :], inc="dve")
                    guards[("psa", bank)] = ("dve", p.val("dve"))
                return p.val("dve")

            v_ksb = proj_2048(0, k_sb, stream_qkv(wk_d, 0))
            v_vsb = proj_2048(1, v_sb, stream_qkv(wv_d, 1))
            p.wait("tensor", "dve", v_vsb)
            for rnd in range(2):
                guard_wait("tensor", "psx")
                for t8 in range(8):
                    mc = rnd * 8 + t8
                    p.op("tensor", "transpose", psx[:, t8 * 128:(t8 + 1) * 128],
                         v_sb[:, mc * 128:(mc + 1) * 128], ident_b[:])
                v = pe_flush()
                p.wait("vector", "pe", v)
                p.op("vector", "tensor_copy", vt_sb[:, rnd * 1024:(rnd + 1) * 1024],
                     psx[:], inc="dve")
                guards["psx"] = ("dve", p.val("dve"))
            v_vt_pe = p.val("pe")
            v_vt = p.val("dve")
            if "kv" in debug:
                p.wait("gpsimd", "dve", v_vt)
                p.op("gpsimd", "dma_start", out=dbg_d["kv"][:, 0, :], in_=k_sb, inc="dg")
                p.op("gpsimd", "dma_start", out=dbg_d["kv"][:, 1, :], in_=vt_sb, inc="dg")

            def proj_q(h):
                slot = h % 2
                vr = stream_qkv(wq_d[h], slot)
                p.wait("tensor", "act", vr)
                for nb in range(TBN):
                    guard_wait("tensor", ("psa", 5))
                    for c in range(DC):
                        p.op("tensor", "matmul", psa[:, 5, :],
                             wqkv_bf[:, slot, c, :],
                             xn_fm[:, c, nb * NT:(nb + 1) * NT],
                             start=(c == 0), stop=(c == DC - 1),
                             inc="pe" if c == DC - 1 else None)
                    v = p.val("pe")
                    guards[("wqkv_bf", slot)] = ("pe", v)
                    p.wait("vector", "pe", v)
                    guard_wait("vector", ("q_sb", h % 2))
                    p.op("vector", "tensor_copy", q_sb[:, h % 2, nb * NT:(nb + 1) * NT],
                         psa[:, 5, :], inc="dve")
                    guards[("psa", 5)] = ("dve", p.val("dve"))
                return p.val("dve")

            guards[("q_sb", 0)] = ("pe", v_vt_pe)
            guards[("psa", "lt", 0)] = guards.get(("psa", 0))
            guards[("psa", "lt", 1)] = guards.get(("psa", 1))
            q_ready = proj_q(0)
            guards[("psa", "lt", 2)] = guards.get(("psa", 5))

            pending = [None]

            def emit_finisher(fin):
                nb, otbank, v_recip = fin
                p.wait("tensor", "dve", v_recip)
                guard_wait("tensor", ("psa", 6))
                p.op("tensor", "matmul", psa[:, 6, :], ones_row[:], sumrec,
                     start=True, stop=True, inc="pe")
                v_rb = p.val("pe")
                guards["sumrec"] = ("pe", v_rb)
                p.wait("vector", "pe", v_rb)
                p.op("vector", "tensor_copy", rb_sb[:, nb % 2, :], psa[:, 6, :], inc="dve")
                guards[("psa", 6)] = ("dve", p.val("dve"))
                p.wait("vector", "dve", p.val("dve"))
                h_, ocat_ap = fin_ctx[nb]
                p.op("vector", "tensor_tensor", ocat_ap,
                     psa[:, otbank, :], rb_sb[:, nb % 2, :], ALU.mult, inc="dve")
                guards[("psa", otbank)] = ("dve", p.val("dve"))

            fin_ctx = {}
            LTB = (0, 1, 5)
            for h in range(NH):
                p.wait("tensor", "dve", q_ready)

                def emit_L(j):
                    nb_, mc_ = divmod(j, 16)
                    guard_wait("tensor", ("psa", "lt", j % 3))
                    p.op("tensor", "matmul", psa[:, LTB[j % 3], :],
                         k_sb[:, mc_ * 128:(mc_ + 1) * 128],
                         q_sb[:, h % 2, nb_ * NT:(nb_ + 1) * NT],
                         start=True, stop=True, inc="pe")
                    return p.val("pe")

                v_l = {0: emit_L(0), 1: emit_L(1)}

                def emit_exp(j):
                    p.wait("scalar", "pe", v_l[j])
                    guard_wait("scalar", ("e_sb", j % 2))
                    p.op("scalar", "activation", e_sb[:, j % 2, :], psa[:, LTB[j % 3], :],
                         AF.Exp, inc="act")
                    v_e = p.val("act")
                    guards[("psa", "lt", j % 3)] = ("act", v_e)
                    return v_e

                def emit_O(j, v_e):
                    nb_, mc_ = divmod(j, 16)
                    otb = 2 + nb_ % 2
                    if mc_ == 0:
                        guard_wait("tensor", ("psa", otb), ("psa", 4))
                    p.wait("tensor", "act", v_e)
                    p.op("tensor", "matmul", psa[:, otb, :],
                         vt_sb[:, mc_ * 128:(mc_ + 1) * 128], e_sb[:, j % 2, :],
                         start=(mc_ == 0), stop=(mc_ == 15))

                def emit_S(j):
                    mc_ = j % 16
                    p.op("tensor", "matmul", psa[0:1, 4, :],
                         ones_col[:], e_sb[:, j % 2, :],
                         start=(mc_ == 0), stop=(mc_ == 15), inc="pe")
                    guards[("e_sb", j % 2)] = ("pe", p.val("pe"))

                for jp in range(0, 64, 2):
                    j0, j1 = jp, jp + 1
                    nb = j0 // 16
                    otbank = 2 + nb % 2
                    if j0 + 2 < 64:
                        v_l[j0 + 2] = emit_L(j0 + 2)
                    if j0 % 16 == 0 and pending[0] is not None:
                        emit_finisher(pending[0])
                        pending[0] = None
                    ve0 = emit_exp(j0)
                    emit_O(j0, ve0)
                    if j1 + 2 < 64:
                        v_l[j1 + 2] = emit_L(j1 + 2)
                    ve1 = emit_exp(j1)
                    emit_O(j1, ve1)
                    emit_S(j0)
                    emit_S(j1)
                    if j1 % 16 == 15:
                        v_osum = p.val("pe")
                        p.wait("vector", "pe", v_osum)
                        guard_wait("vector", "sumrec")
                        p.op("vector", "reciprocal", sumrec, psa[0:1, 4, :], inc="dve")
                        v_recip = p.val("dve")
                        guards[("psa", 4)] = ("dve", v_recip)
                        fin_ctx[nb] = (h, ocat[:, h, nb * NT:(nb + 1) * NT])
                        pending[0] = (nb, otbank, v_recip)
                guards[("q_sb", h % 2)] = ("pe", p.val("pe"))
                if h + 1 < NH:
                    guards[("psa", 5)] = guards.get(("psa", "lt", 2))
                    q_ready = proj_q(h + 1)
                    guards[("psa", "lt", 2)] = guards.get(("psa", 5))
            emit_finisher(pending[0])
            pending[0] = None
            v_attn_pe = p.val("pe")
            v_ocat = p.val("dve")
            if "ocat" in debug:
                p.wait("gpsimd", "dve", v_ocat)
                p.op("gpsimd", "dma_start", out=dbg_d["ocat"],
                     in_=ocat.rearrange("p h n -> p (h n)"), inc="dg")

            wo_slot = [0]
            for hh in range(NH):
                slot = wo_slot[0]; wo_slot[0] ^= 1
                guard_wait("sync", ("wo_st", slot))
                dv = p.op("sync", "dma_start", out=wo_st[:, slot, :], in_=wot_d[hh], inc=f"dc{slot}")
                p.wait("scalar", f"dc{slot}", dv)
                if hh == 0:
                    p.wait("scalar", "pe", v_attn_pe)
                p.op("scalar", "activation", wo_bf[:, hh, :], wo_st[:, slot, :], AF.Copy, inc="act")
                guards[("wo_st", slot)] = ("act", p.val("act"))
            v_wo = p.val("act")

            out_rows = out_d.rearrange("(t p) d -> t p d", p=128)
            xtok_dma = {}
            for ns in range(2):
                xtok_dma[ns] = p.op("sync", "dma_start", out=xtokC[:, ns % 2, :],
                                    in_=x_rows[ns], inc=f"ds{ns % 2}")
            p.wait("tensor", "act", v_wo)
            p.wait("tensor", "dve", v_ocat)
            guards[("psa", 0)] = guards.get(("psa", "lt", 0))
            guards[("psa", 1)] = guards.get(("psa", "lt", 1))
            for ns in range(NTT):
                yb = 2 * (ns % 2)
                gbank = 4 + ns % 2
                guard_wait("tensor", ("psa", yb), ("psa", yb + 1))
                for hh in range(NH):
                    for half in range(2):
                        p.op("tensor", "matmul", psa[:, yb + half, :],
                             ocat[:, hh, ns * 128:(ns + 1) * 128],
                             wo_bf[:, hh, half * 512:(half + 1) * 512],
                             start=(hh == 0), stop=(hh == NH - 1),
                             inc="pe" if (hh == NH - 1 and half == 1) else None)
                v_y = p.val("pe")
                for half in range(2):
                    guard_wait("tensor", ("psa", gbank))
                    for c4 in range(4):
                        c = half * 4 + c4
                        p.op("tensor", "transpose", psa[:, gbank, c4 * 128:(c4 + 1) * 128],
                             result[:, c, ns * 128:(ns + 1) * 128], ident_f[:])
                    v_t = pe_flush()
                    p.wait("vector", "pe", v_t)
                    guard_wait("vector", ("gt", ns % 2))
                    p.op("vector", "tensor_copy",
                         gt_sb[:, ns % 2, half * 512:(half + 1) * 512],
                         psa[:, gbank, :], inc="dve")
                    guards[("psa", gbank)] = ("dve", p.val("dve"))
                p.wait("vector", "pe", v_y)
                p.wait("vector", f"ds{ns % 2}", xtok_dma[ns])
                p.wait("vector", "dve", p.val("dve"))
                yview = psa[:, yb:yb + 2, :].rearrange("p a b -> p (a b)")
                p.op("vector", "tensor_tensor", gt_sb[:, ns % 2, :], yview,
                     gt_sb[:, ns % 2, :], ALU.mult, inc="dve")
                p.wait("vector", "dve", p.val("dve"))
                p.op("vector", "tensor_tensor", xtokC[:, ns % 2, :], xtokC[:, ns % 2, :],
                     gt_sb[:, ns % 2, :], ALU.add, inc="dve")
                v_ep = p.val("dve")
                guards[("psa", yb)] = ("dve", v_ep)
                guards[("psa", yb + 1)] = ("dve", v_ep)
                guards[("gt", ns % 2)] = ("dve", v_ep)
                p.wait("sync", "dve", v_ep)
                p.op("sync", "dma_start", out=out_rows[ns], in_=xtokC[:, ns % 2, :], inc="dc")
                guards[("xtokC", ns % 2)] = ("dc", p.val("dc"))
                if ns + 2 < NTT:
                    guard_wait("sync", ("xtokC", ns % 2))
                    xtok_dma[ns + 2] = p.op("sync", "dma_start", out=xtokC[:, ns % 2, :],
                                            in_=x_rows[ns + 2], inc=f"ds{ns % 2}")
            p.wait("scalar", "dc", p.val("dc"))
            if debug:
                p.wait("gpsimd", "dg", p.val("dg"))
        else:
            if debug:
                p.wait("gpsimd", "dg", p.val("dg"))
            p.wait("gpsimd", "dve", p.val("dve"))
            p.op("gpsimd", "dma_start",
                 out=out_d.rearrange("(c p) d -> p c d", p=128),
                 in_=result[:].rearrange("p c n -> p (c n)").rearrange("p (a b) -> p a b", a=16),
                 inc="dg")
            p.wait("gpsimd", "dg", p.val("dg"))

        p.check()

        with nc.Block() as block:
            @block.tensor
            def _(h):
                for f in p.ops["tensor"]:
                    f(h, sems)

            @block.vector
            def _(h):
                for f in p.ops["vector"]:
                    f(h, sems)

            @block.scalar
            def _(h):
                for f in p.ops["scalar"]:
                    f(h, sems)

            @block.gpsimd
            def _(h):
                for f in p.ops["gpsimd"]:
                    f(h, sems)

            @block.sync
            def _(h):
                for f in p.ops["sync"]:
                    f(h, sems)

    return nc


def make_in_maps(L0_attn_x, L0_attn_proj_q, L0_attn_proj_k, L0_attn_proj_v, L0_attn_proj_o,
                 L0_ffn_gate_w, L0_ffn_gate_b, L0_ffn_sh_w1, L0_ffn_sh_b1, L0_ffn_sh_w2,
                 L0_ffn_sh_b2, L0_ffn_ew1, L0_ffn_eb1, L0_ffn_ew2, L0_ffn_eb2):
    f32 = np.float32
    common = {
        "wq": np.ascontiguousarray(L0_attn_proj_q, f32),
        "wk": np.ascontiguousarray(L0_attn_proj_k, f32),
        "wv": np.ascontiguousarray(L0_attn_proj_v, f32),
        "wot": np.ascontiguousarray(np.asarray(L0_attn_proj_o, f32).transpose(0, 2, 1)),
        "gw": np.ascontiguousarray(L0_ffn_gate_w, f32),
        "gb": np.ascontiguousarray(L0_ffn_gate_b, f32),
        "shw1": np.ascontiguousarray(L0_ffn_sh_w1, f32),
        "shb1": np.ascontiguousarray(L0_ffn_sh_b1, f32),
        "shw2": np.ascontiguousarray(L0_ffn_sh_w2, f32),
        "shb2": np.ascontiguousarray(L0_ffn_sh_b2, f32),
        "ew1": np.ascontiguousarray(L0_ffn_ew1, f32),
        "eb1": np.ascontiguousarray(L0_ffn_eb1, f32),
        "ew2": np.ascontiguousarray(L0_ffn_ew2, f32),
        "eb2": np.ascontiguousarray(L0_ffn_eb2, f32),
    }
    xs = np.ascontiguousarray(L0_attn_x, f32)
    return [dict(common, x=xs[c]) for c in range(8)]


def kernel(**inputs):
    nc = build_nc()
    in_maps = make_in_maps(**inputs)
    res = run_bass_kernel_spmd(nc, in_maps, core_ids=list(range(8)))
    return np.stack([res.results[c]["out"] for c in range(8)], axis=0)


# version 09
# speedup vs baseline: 1.1883x; 1.1883x over previous
import contextlib
import numpy as np
import concourse.bass as bass
import concourse.mybir as mybir
from concourse.bass_utils import run_bass_kernel_spmd

F32 = mybir.dt.float32
BF16 = mybir.dt.bfloat16
AF = mybir.ActivationFunctionType
ALU = mybir.AluOpType
AX = mybir.AxisListType.X

S = 2048
D = 1024
DC = 8
NH = 8
FI = 4096
NE = 8
NU = NE + 1
NTT = 16
NT = 512
TBN = 4
NIG = 8
ICG = 4
EPS_LN = 1e-5
EPS_RT = 1e-9

ENGS = ("tensor", "vector", "scalar", "gpsimd", "sync")
DMA_SEMS = {"ds", "dc", "dg", "ds0", "ds1", "dc0", "dc1", "dgb"}
ALL_SEMS = ("pe", "act", "dve", "pool", "ds", "dc", "dg",
            "ds0", "ds1", "dc0", "dc1", "dgb")


class Plan:

    def __init__(self):
        self.ops = {e: [] for e in ENGS}
        self.cnt = {}
        self.waits = []
        self.last_wait = {}

    def op(self, eng, meth, *args, inc=None, **kwargs):
        step = 16 if inc in DMA_SEMS else 1
        if inc is not None:
            self.cnt[inc] = self.cnt.get(inc, 0) + step
        v = self.cnt.get(inc) if inc is not None else None

        def f(h, sems, meth=meth, args=args, kwargs=kwargs, inc=inc, step=step):
            ins = getattr(h, meth)(*args, **kwargs)
            if inc is not None:
                ins.then_inc(sems[inc], step)
        self.ops[eng].append(f)
        return v

    def wait(self, eng, sem, val):
        if val is None or val <= 0:
            return
        if self.last_wait.get((eng, sem), -1) >= val:
            return
        self.last_wait[(eng, sem)] = val
        self.waits.append((sem, val))

        def f(h, sems, sem=sem, val=val):
            h.wait_ge(sems[sem], val)
        self.ops[eng].append(f)

    def val(self, sem):
        return self.cnt.get(sem, 0)

    def check(self):
        for sem, val in self.waits:
            assert val <= self.cnt.get(sem, 0), \
                f"wait {sem}>={val} never satisfied (final {self.cnt.get(sem, 0)})"


def build_nc(phases=3, debug=()):
    nc = bass.Bass("TRN2", target_bir_lowering=False)
    debug = set(debug)

    ap = lambda t: t.ap()
    x_d = ap(nc.dram_tensor("x", [S, D], F32, kind="ExternalInput"))
    wq_d = ap(nc.dram_tensor("wq", [NH, D, 128], F32, kind="ExternalInput"))
    wk_d = ap(nc.dram_tensor("wk", [D, 128], F32, kind="ExternalInput"))
    wv_d = ap(nc.dram_tensor("wv", [D, 128], F32, kind="ExternalInput"))
    wot_d = ap(nc.dram_tensor("wot", [NH, 128, D], F32, kind="ExternalInput"))
    gw_d = ap(nc.dram_tensor("gw", [D, NE], F32, kind="ExternalInput"))
    gb_d = ap(nc.dram_tensor("gb", [NE], F32, kind="ExternalInput"))
    w1_d = [ap(nc.dram_tensor("shw1", [D, FI], F32, kind="ExternalInput"))]
    b1_d = [ap(nc.dram_tensor("shb1", [FI], F32, kind="ExternalInput"))]
    w2_d = [ap(nc.dram_tensor("shw2", [FI, D], F32, kind="ExternalInput"))]
    b2_d = [ap(nc.dram_tensor("shb2", [D], F32, kind="ExternalInput"))]
    ew1_d = ap(nc.dram_tensor("ew1", [NE, D, FI], F32, kind="ExternalInput"))
    eb1_d = ap(nc.dram_tensor("eb1", [NE, FI], F32, kind="ExternalInput"))
    ew2_d = ap(nc.dram_tensor("ew2", [NE, FI, D], F32, kind="ExternalInput"))
    eb2_d = ap(nc.dram_tensor("eb2", [NE, D], F32, kind="ExternalInput"))
    for e in range(NE):
        w1_d.append(ew1_d[e])
        b1_d.append(eb1_d[e])
        w2_d.append(ew2_d[e])
        b2_d.append(eb2_d[e])
    out_d = ap(nc.dram_tensor("out", [S, D], F32, kind="ExternalOutput"))
    dbg_d = {}
    if debug & {"resA", "ffn", "g"}:
        dbg_d["res"] = ap(nc.dram_tensor("dbg_res", [128, DC, S], F32, kind="ExternalOutput"))
    if "router" in debug:
        dbg_d["wet"] = ap(nc.dram_tensor("dbg_wet", [NE, S], F32, kind="ExternalOutput"))
    if "router2" in debug:
        dbg_d["gbp"] = ap(nc.dram_tensor("dbg_gbp", [8, 1], F32, kind="ExternalOutput"))
        dbg_d["grow"] = ap(nc.dram_tensor("dbg_grow", [1, 8], F32, kind="ExternalOutput"))
        dbg_d["gsb"] = ap(nc.dram_tensor("dbg_gsb", [8, S], F32, kind="ExternalOutput"))
        dbg_d["rtok"] = ap(nc.dram_tensor("dbg_rtok", [128, 128], F32, kind="ExternalOutput"))
        dbg_d["wtok"] = ap(nc.dram_tensor("dbg_wtok", [128, 128], F32, kind="ExternalOutput"))
    if "ocat" in debug:
        dbg_d["ocat"] = ap(nc.dram_tensor("dbg_ocat", [128, NH * S], BF16, kind="ExternalOutput"))
    if "kv" in debug:
        dbg_d["kv"] = ap(nc.dram_tensor("dbg_kv", [128, 2, S], BF16, kind="ExternalOutput"))

    p = Plan()

    with contextlib.ExitStack() as ctx:
        E = ctx.enter_context
        result = E(nc.sbuf_tensor("result", [128, DC, S], F32))
        xn_fm = E(nc.sbuf_tensor("xn_fm", [128, DC, S], BF16))
        abf = E(nc.sbuf_tensor("abf", [128, 27648], BF16))
        af = E(nc.sbuf_tensor("af", [128, 13312], F32))
        ident_f = E(nc.sbuf_tensor("ident_f", [128, 128], F32))
        ident_b = E(nc.sbuf_tensor("ident_b", [128, 128], BF16))
        ones_col = E(nc.sbuf_tensor("ones_col", [128, 1], BF16))
        ones_row = E(nc.sbuf_tensor("ones_row", [1, 128], F32))
        ebias1 = E(nc.sbuf_tensor("ebias1", [128, NU * 32], F32))
        ebias2 = E(nc.sbuf_tensor("ebias2", [128, NU * 8], F32))
        gatebp = E(nc.sbuf_tensor("gatebp", [128, 1], F32))
        gate_st = E(nc.sbuf_tensor("gate_st", [128, 8, 8], F32))
        rtk = E(nc.sbuf_tensor("rtk", [128, 16], F32))

        bf = abf[:]
        fa = af[:]
        xtok4v = fa[:, 4096:8192].rearrange("p (b n) -> p b n", b=4)
        sqN = fa[:, 8192:10240].rearrange("p (b n) -> p b n", b=2)
        xnfN = fa[:, 10240:12288].rearrange("p (b n) -> p b n", b=2)
        statA = fa[:, 12288:12416]
        hi_tokN = bf[:, 16384:18432].rearrange("p (b n) -> p b n", b=2)
        lo_tokN = bf[:, 18432:20480].rearrange("p (b n) -> p b n", b=2)
        lo_fmN = bf[:, 20480:22528].rearrange("p (b c n) -> p b c n", b=2, c=DC)
        wstage = fa[:, 0:4096].rearrange("p (b n) -> p b n", b=2)
        wrowstage = fa[0:1, 0:4096]
        wbc = fa[:, 4096:8192].rearrange("p (b n) -> p b n", b=2)
        g1 = fa[:, 8192:9216].rearrange("p (b n) -> p b n", b=2)
        rtok = fa[:, 8192:8320]
        wtok = fa[:, 8320:8448]
        wrow = fa[0:1, 9216:11264]
        wet = fa[0:8, 11264:13312]
        w1bf = bf[:, 0:8192].rearrange("p (b c i) -> p b c i", b=2, c=DC)
        w2bf = bf[:, 8192:16384].rearrange("p (b j d) -> p b j d", b=2, j=ICG)
        h_sb = bf[:, 16384:24576].rearrange("p (j n) -> p j n", j=ICG)
        ocat = bf[:, 0:16384].rearrange("p (h n) -> p h n", h=NH)
        k_sb = bf[:, 16384:18432]
        vt_sb = bf[:, 18432:20480]
        v_sb = bf[:, 20480:22528]
        q_sb = bf[:, 20480:24576].rearrange("p (b n) -> p b n", b=2)
        e_sb = bf[:, 24576:25600].rearrange("p (b n) -> p b n", b=2)
        wqkv_bf = bf[:, 25600:27648].rearrange("p (b c k) -> p b c k", b=2, c=DC)
        wo_bf = bf[:, 16384:24576].rearrange("p (h d) -> p h d", h=NH)
        rb_sb = fa[:, 0:1024].rearrange("p (b n) -> p b n", b=2)
        gt_sb = fa[:, 1024:3072].rearrange("p (b n) -> p b n", b=2)
        xtokC = fa[:, 3072:5120].rearrange("p (b n) -> p b n", b=2)
        wqkv_st = fa[:, 5120:7168].rearrange("p (b c k) -> p b c k", b=2, c=DC)
        wo_st = fa[:, 7168:9216].rearrange("p (b n) -> p b n", b=2)
        sumrec = fa[0:1, 9216:9728]

        psa = E(nc.psum_tensor("psa", [128, 7, 512], F32))
        psx = E(nc.psum_tensor("psx", [128, 1024], BF16))

        sems = {}
        for s_ in ALL_SEMS:
            sems[s_] = E(nc.semaphore(s_))

        guards = {}

        def guard_wait(eng, *keys):
            for key in keys:
                g = guards.get(key)
                if g:
                    p.wait(eng, g[0], g[1])

        def barrier(skip=()):
            vals = [(s_, p.val(s_)) for s_ in ALL_SEMS if s_ not in skip]
            for e_ in ENGS:
                for s_, v_ in vals:
                    p.wait(e_, s_, v_)

        def pe_flush():
            p.op("tensor", "matmul", psa[0:128, 6, 256:384], ident_b[:],
                 ident_b[:], start=True, stop=True)
            p.op("tensor", "matmul", psa[0:128, 6, 384:512], ident_b[:],
                 ident_b[:], start=True, stop=True, inc="pe")
            return p.val("pe")

        p.op("gpsimd", "memset", ident_f[:], 0.0)
        p.op("gpsimd", "memset", ident_b[:], 0.0, inc="pool")
        p.wait("gpsimd", "pool", p.val("pool"))
        p.op("gpsimd", "affine_select", out=ident_f[:], in_=ident_f[:],
             compare_op=ALU.not_equal, fill=1.0, base=0, pattern=[[-1, 128]], channel_multiplier=1)
        p.op("gpsimd", "affine_select", out=ident_b[:], in_=ident_b[:],
             compare_op=ALU.not_equal, fill=1.0, base=0, pattern=[[-1, 128]], channel_multiplier=1)
        p.op("gpsimd", "memset", ones_col[:], 1.0)
        p.op("gpsimd", "memset", statA[:, 64:65], EPS_LN)
        p.op("gpsimd", "memset", ones_row[:], 1.0, inc="pool")
        v_const = p.val("pool")
        p.wait("tensor", "pool", v_const)
        p.wait("scalar", "pool", v_const)

        p.op("gpsimd", "dma_start", out=gatebp[0:8, :],
             in_=gb_d.rearrange("(p a) -> p a", a=1),
             allow_slow_non_contiguous=True, inc="dgb")
        v_gatebp_dma = p.val("dgb")
        vb1 = {}
        vb2 = {}
        for u in range(NU):
            p.op("gpsimd", "dma_start", out=ebias1[:, u * 32:(u + 1) * 32],
                 in_=b1_d[u].rearrange("(c p) -> p c", p=128),
                 allow_slow_non_contiguous=True, inc="dgb")
            vb1[u] = p.val("dgb")
            p.op("gpsimd", "dma_start", out=ebias2[:, u * 8:(u + 1) * 8],
                 in_=b2_d[u].rearrange("(c p) -> p c", p=128),
                 allow_slow_non_contiguous=True, inc="dgb")
            vb2[u] = p.val("dgb")

        wbc_pool_guard = {}
        wrow_dv = {}
        wbc_ready = {}
        v_wet_h = [None]

        def wbc_dma(e):
            guard_wait("gpsimd", "wrow")
            p.wait("gpsimd", "dve", v_wet_h[0])
            wrow_dv[e] = p.op("gpsimd", "dma_start", out=wrow, in_=wet[e:e + 1, :], inc="dg")

        def build_wbc(e):
            if e not in wrow_dv:
                wbc_dma(e)
            dv = wrow_dv[e]
            p.wait("tensor", "dg", dv)
            pg = wbc_pool_guard.get(e % 2)
            if pg:
                p.wait("vector", "pool", pg)
            for tb in range(TBN):
                guard_wait("tensor", ("psa", 6))
                p.op("tensor", "matmul", psa[:, 6, :], ones_row[:],
                     wrow[:, tb * NT:(tb + 1) * NT], start=True, stop=True, inc="pe")
                v = p.val("pe")
                guards["wrow"] = ("pe", v)
                p.wait("vector", "pe", v)
                guard_wait("vector", ("wbc", e % 2))
                p.op("vector", "tensor_copy", wbc[:, e % 2, tb * NT:(tb + 1) * NT],
                     psa[:, 6, :], inc="dve")
                guards[("psa", 6)] = ("dve", p.val("dve"))
            return p.val("dve")

        wstage_slot = [0]

        def dma_w1(u, ig):
            dvs = []
            for half in range(2):
                slot = wstage_slot[0]; wstage_slot[0] ^= 1
                guard_wait("sync", ("wstage", slot))
                src = w1_d[u].rearrange("(c p) i -> p c i", p=128)[
                    :, :, ig * 512 + half * 256: ig * 512 + (half + 1) * 256]
                dv = p.op("sync", "dma_start",
                          out=wstage[:, slot, :].rearrange("p (c i) -> p c i", c=DC),
                          in_=src, inc=f"ds{slot}")
                dvs.append((slot, dv))
            return dvs

        def conv_w1(ig, dvs):
            buf = ig % 2
            guard_wait("scalar", ("w1bf", buf))
            for half, (slot, dv) in enumerate(dvs):
                p.wait("scalar", f"ds{slot}", dv)
                p.op("scalar", "activation",
                     w1bf[:, buf, :, half * 256:(half + 1) * 256],
                     wstage[:, slot, :].rearrange("p (c i) -> p c i", c=DC), AF.Copy, inc="act")
                guards[("wstage", slot)] = ("act", p.val("act"))
            return p.val("act")

        def dma_w2(u, ig):
            dvs = []
            for half in range(2):
                slot = wstage_slot[0]; wstage_slot[0] ^= 1
                guard_wait("sync", ("wstage", slot))
                src = w2_d[u].rearrange("(g p) d -> p g d", p=128)[
                    :, ig * ICG + half * 2: ig * ICG + (half + 1) * 2, :]
                dv = p.op("sync", "dma_start",
                          out=wstage[:, slot, :].rearrange("p (g d) -> p g d", g=2),
                          in_=src, inc=f"ds{slot}")
                dvs.append((slot, dv))
            return dvs

        def conv_w2(ig, dvs):
            buf = ig % 2
            guard_wait("scalar", ("w2bf", buf))
            for half, (slot, dv) in enumerate(dvs):
                p.wait("scalar", f"ds{slot}", dv)
                p.op("scalar", "activation",
                     w2bf[:, buf, half * 2:(half + 1) * 2, :],
                     wstage[:, slot, :].rearrange("p (j d) -> p j d", j=2), AF.Copy, inc="act")
                guards[("wstage", slot)] = ("act", p.val("act"))
            return p.val("act")

        x_rows = x_d.rearrange("(t p) d -> t p d", p=128)
        s1 = statA[:, 0:16]; s2 = statA[:, 16:32]
        rstd = statA[:, 32:48]; nmr = statA[:, 48:64]
        eps_col = statA[:, 64:65]

        def vfence():
            p.wait("vector", "dve", p.val("dve"))

        if phases >= 2:
            d1 = dma_w1(0, 0)
        xload = {}

        def emit_xload(tt):
            b4 = tt % 4
            guard_wait("scalar", ("xtok4", b4))
            xload[tt] = p.op("scalar", "dma_start", out=xtok4v[:, b4, :],
                             in_=x_rows[tt], inc=f"dc{tt % 2}")

        emit_xload(0)
        emit_xload(1)
        for tt in range(NTT):
            b4 = tt % 4
            b = tt % 2
            if tt + 2 < NTT:
                emit_xload(tt + 2)
            dv = xload[tt]
            xv = xtok4v[:, b4, :]
            sc = slice(tt, tt + 1)
            p.wait("vector", f"dc{b}", dv)
            p.op("vector", "reduce_sum", s1[:, sc], xv, axis=AX, inc="dve")
            p.wait("scalar", f"dc{b}", dv)
            guard_wait("scalar", ("sq", b))
            p.op("scalar", "activation", sqN[:, b, :], xv, AF.Square, inc="act")
            p.wait("vector", "act", p.val("act"))
            p.op("vector", "reduce_sum", s2[:, sc], sqN[:, b, :], axis=AX, inc="dve")
            guards[("sq", b)] = ("dve", p.val("dve"))
            vfence()
            p.op("vector", "tensor_scalar", s1[:, sc], s1[:, sc], 1.0 / D, None, ALU.mult)
            p.op("vector", "tensor_scalar", s2[:, sc], s2[:, sc], 1.0 / D, None, ALU.mult, inc="dve")
            vfence()
            p.op("vector", "tensor_mul", rstd[:, sc], s1[:, sc], s1[:, sc], inc="dve")
            vfence()
            p.op("vector", "tensor_sub", s2[:, sc], s2[:, sc], rstd[:, sc], inc="dve")
            p.wait("scalar", "dve", p.val("dve"))
            p.op("scalar", "activation", s2[:, sc], s2[:, sc], AF.Sqrt, bias=eps_col, inc="act")
            p.wait("vector", "act", p.val("act"))
            p.op("vector", "reciprocal", rstd[:, sc], s2[:, sc], inc="dve")
            vfence()
            p.op("vector", "scalar_tensor_tensor", nmr[:, sc], s1[:, sc], -1.0,
                 rstd[:, sc], ALU.mult, ALU.mult, inc="dve")
            vfence()
            guard_wait("vector", ("xnf", b))
            p.op("vector", "tensor_scalar", xnfN[:, b, :], xv,
                 rstd[:, sc], nmr[:, sc], ALU.mult, ALU.add, inc="dve")
            v_apply = p.val("dve")
            guards[("xtok4", b4)] = ("dve", v_apply)
            p.wait("scalar", "dve", v_apply)
            guard_wait("scalar", ("hi", b), ("hi2", b))
            p.op("scalar", "activation", hi_tokN[:, b, :], xnfN[:, b, :], AF.Copy, inc="act")
            v_hi = p.val("act")
            guard_wait("scalar", ("sq", b))
            p.op("scalar", "activation", sqN[:, b, :], hi_tokN[:, b, :], AF.Copy, inc="act")
            v_hif = p.val("act")
            guards[("hi2", b)] = ("act", v_hif)
            p.wait("vector", "act", v_hif)
            guard_wait("vector", ("lo", b))
            p.op("vector", "tensor_sub", lo_tokN[:, b, :], xnfN[:, b, :], sqN[:, b, :], inc="dve")
            v_lo = p.val("dve")
            guards[("xnf", b)] = ("dve", v_lo)
            guards[("sq", b)] = ("dve", v_lo)
            p.wait("sync", "act", v_hi)
            dv_hi = p.op("sync", "dma_start", out=xn_fm[:, :, tt * 128:(tt + 1) * 128],
                         in_=hi_tokN[:, b, :].rearrange("p (c n) -> p c n", c=DC),
                         transpose=True, inc="ds0")
            guards[("hi", b)] = ("ds0", dv_hi)
            p.wait("sync", "dve", v_lo)
            guard_wait("sync", ("lofm", b))
            dv_lo = p.op("sync", "dma_start", out=lo_fmN[:, b, :, :],
                         in_=lo_tokN[:, b, :].rearrange("p (c n) -> p c n", c=DC),
                         transpose=True, inc="ds1")
            guards[("lo", b)] = ("ds1", dv_lo)
            p.wait("vector", "ds0", dv_hi)
            p.wait("vector", "ds1", dv_lo)
            p.op("vector", "tensor_tensor", result[:, :, tt * 128:(tt + 1) * 128],
                 xn_fm[:, :, tt * 128:(tt + 1) * 128], lo_fmN[:, b, :, :],
                 ALU.add, inc="dve")
            guards[("lofm", b)] = ("dve", p.val("dve"))

        if phases >= 2:
            w1_ready = conv_w1(0, d1)
            d2 = dma_w2(0, 0)
        barrier(skip=("dgb", "ds0", "ds1"))

        if "resA" in debug:
            p.op("gpsimd", "dma_start", out=dbg_d["res"], in_=result[:], inc="dg")

        dvgw = p.op("sync", "dma_start", out=gate_st[:],
                    in_=gw_d.rearrange("(c p) e -> p c e", p=128), inc="dc")
        gate_sb = fa[0:8, 9216:11264]
        def router_p1():
            p.wait("tensor", "dc", dvgw)
            p.wait("vector", "dgb", v_gatebp_dma)
            for tb in range(TBN):
                bank = tb % 2
                guard_wait("tensor", ("psa", bank))
                for c in range(DC):
                    p.op("tensor", "matmul", psa[0:8, bank, :], gate_st[:, c, :],
                         result[:, c, tb * NT:(tb + 1) * NT],
                         start=(c == 0), stop=(c == DC - 1),
                         inc="pe" if c == DC - 1 else None)
                v = p.val("pe")
                p.wait("vector", "pe", v)
                guard_wait("vector", "wrowstage")
                p.op("vector", "tensor_scalar", gate_sb[:, tb * NT:(tb + 1) * NT],
                     psa[0:8, bank, :], gatebp[0:8, :], None, ALU.add, inc="dve")
                guards[("psa", bank)] = ("dve", p.val("dve"))

        def router_p2():
            v_gate = p.val("dve")
            if "router2" in debug:
                p.wait("gpsimd", "dve", v_gate)
                p.op("gpsimd", "dma_start", out=dbg_d["gbp"], in_=gatebp[0:8, :], inc="dg")
                p.op("gpsimd", "dma_start", out=dbg_d["gsb"], in_=gate_sb, inc="dg")
            p.wait("tensor", "dve", v_gate)
            for tt in range(NTT):
                bank = 2 + (tt % 2)
                guard_wait("tensor", ("psa", bank))
                p.op("tensor", "transpose", psa[0:128, bank, 0:8],
                     gate_sb[:, tt * 128:(tt + 1) * 128], ident_f[0:8, 0:8])
                v = pe_flush()
                p.wait("vector", "pe", v)
                p.op("vector", "tensor_copy", rtok[:, tt * 8:(tt + 1) * 8], psa[:, bank, 0:8], inc="dve")
                guards[("psa", bank)] = ("dve", p.val("dve"))
            for tt in range(NTT):
                r8 = rtok[:, tt * 8:(tt + 1) * 8]
                w8 = wtok[:, tt * 8:(tt + 1) * 8]
                c = rtk
                def vf():
                    p.wait("vector", "dve", p.val("dve"))
                p.op("vector", "reduce_max", c[:, 0:1], r8, axis=AX, negate=True, inc="dve")
                p.wait("scalar", "dve", p.val("dve"))
                p.op("scalar", "activation", w8, r8, AF.Exp, bias=c[:, 0:1], inc="act")
                p.wait("vector", "act", p.val("act"))
                p.op("vector", "reduce_sum", c[:, 1:2], w8, axis=AX, inc="dve")
                vf()
                p.op("vector", "reciprocal", c[:, 2:3], c[:, 1:2], inc="dve")
                vf()
                p.op("vector", "tensor_scalar", r8, w8, c[:, 2:3], None, ALU.mult, inc="dve")
                vf()
                p.op("vector", "reduce_max", c[:, 3:4], r8, axis=AX, inc="dve")
                vf()
                p.op("vector", "tensor_scalar", c[:, 8:16], r8, c[:, 3:4], None, ALU.is_equal, inc="dve")
                vf()
                p.op("vector", "scalar_tensor_tensor", w8, c[:, 8:16], -2.0, r8, ALU.mult, ALU.add, inc="dve")
                vf()
                p.op("vector", "reduce_max", c[:, 4:5], w8, axis=AX, inc="dve")
                vf()
                p.op("vector", "tensor_scalar", w8, r8, c[:, 4:5], None, ALU.is_ge, inc="dve")
                p.op("vector", "tensor_add", c[:, 5:6], c[:, 3:4], c[:, 4:5], inc="dve")
                vf()
                p.op("vector", "tensor_scalar", c[:, 5:6], c[:, 5:6], EPS_RT, None, ALU.add, inc="dve")
                vf()
                p.op("vector", "reciprocal", c[:, 6:7], c[:, 5:6], inc="dve")
                p.op("vector", "tensor_mul", w8, w8, r8, inc="dve")
                vf()
                p.op("vector", "tensor_scalar", w8, w8, c[:, 6:7], None, ALU.mult, inc="dve")

        def router_p3():
            v_wtok = p.val("dve")
            if "router2" in debug:
                p.wait("gpsimd", "dve", v_wtok)
                p.op("gpsimd", "dma_start", out=dbg_d["rtok"], in_=rtok, inc="dg")
                p.op("gpsimd", "dma_start", out=dbg_d["wtok"], in_=wtok, inc="dg")
            p.wait("tensor", "dve", v_wtok)
            for tt in range(NTT):
                bank = 2 + (tt % 2)
                guard_wait("tensor", ("psa", bank))
                p.op("tensor", "transpose", psa[0:8, bank, 0:128],
                     wtok[:, tt * 8:(tt + 1) * 8], ident_f[:])
                v = pe_flush()
                p.wait("vector", "pe", v)
                p.op("vector", "tensor_copy", wet[:, tt * 128:(tt + 1) * 128],
                     psa[0:8, bank, 0:128], inc="dve")
                guards[("psa", bank)] = ("dve", p.val("dve"))
            v_wet = p.val("dve")
            if "router" in debug:
                p.wait("gpsimd", "dve", v_wet)
                p.op("gpsimd", "dma_start", out=dbg_d["wet"], in_=wet, inc="dg")
            return v_wet

        if phases >= 2:
            gelu_j = [0]
            units = [(u, ig) for u in range(NU) for ig in range(NIG)]
            for idx, (u, ig) in enumerate(units):
                is_exp = u >= 1
                e = u - 1
                buf = ig % 2
                if ig == 0:
                    p.wait("scalar", "dgb", vb1[u])
                    p.wait("vector", "dgb", vb2[u])
                if is_exp and ig == 0:
                    p.wait("gpsimd", "dve", wbc_ready[e])
                w2_ready = conv_w2(ig, d2)
                nxt = units[idx + 1] if idx + 1 < len(units) else None
                if nxt:
                    d1 = dma_w1(*nxt)
                p.wait("tensor", "act", w1_ready)
                h_done = {}
                for ic in range(ICG):
                    icg = ig * ICG + ic
                    for tb in range(TBN):
                        bank = (ic * TBN + tb) % 3
                        guard_wait("tensor", ("psa", bank))
                        for c in range(DC):
                            p.op("tensor", "matmul", psa[:, bank, :],
                                 w1bf[:, buf, c, ic * 128:(ic + 1) * 128],
                                 xn_fm[:, c, tb * NT:(tb + 1) * NT],
                                 start=(c == 0), stop=(c == DC - 1),
                                 inc="pe" if c == DC - 1 else None)
                        v_h = p.val("pe")
                        guards[("w1bf", buf)] = ("pe", v_h)
                        p.wait("scalar", "pe", v_h)
                        bias_ap = ebias1[:, u * 32 + icg:u * 32 + icg + 1]
                        if is_exp:
                            j = gelu_j[0] % 2
                            gelu_j[0] += 1
                            guard_wait("scalar", ("g1", j))
                            p.op("scalar", "activation", g1[:, j, :], psa[:, bank, :],
                                 AF.Gelu_apprx_tanh, bias=bias_ap, inc="act")
                            v_gl = p.val("act")
                            guards[("psa", bank)] = ("act", v_gl)
                            p.wait("gpsimd", "act", v_gl)
                            guard_wait("gpsimd", "h_sb")
                            p.op("gpsimd", "tensor_mul", h_sb[:, ic, tb * NT:(tb + 1) * NT],
                                 g1[:, j, :], wbc[:, e % 2, tb * NT:(tb + 1) * NT], inc="pool")
                            guards[("g1", j)] = ("pool", p.val("pool"))
                            h_done[(ic, tb)] = ("pool", p.val("pool"))
                        else:
                            guard_wait("scalar", "h_sb")
                            p.op("scalar", "activation", h_sb[:, ic, tb * NT:(tb + 1) * NT],
                                 psa[:, bank, :], AF.Gelu_apprx_tanh, bias=bias_ap, inc="act")
                            guards[("psa", bank)] = ("act", p.val("act"))
                            h_done[(ic, tb)] = ("act", p.val("act"))
                if is_exp:
                    wbc_pool_guard[e % 2] = p.val("pool")
                if u == 0:
                    if ig == 0:
                        router_p1()
                    elif ig == 1:
                        router_p2()
                    elif ig == 2:
                        v_wet_h[0] = router_p3()
                if nxt:
                    w1_ready = conv_w1(nxt[1], d1)
                    d2 = dma_w2(*nxt)
                if u + 1 < NU and ig == (2 if u == 0 else 6):
                    pass
                if u + 1 < NU and ig == (3 if u == 0 else 0):
                    wbc_ready[u] = build_wbc(u)
                if u + 1 < NU and ig == (2 if u == 0 else 5):
                    wbc_dma(u)
                p.wait("tensor", "act", w2_ready)
                for dcc in range(DC):
                    for tb in range(TBN):
                        bank = 3 + (dcc * TBN + tb) % 3
                        guard_wait("tensor", ("psa", bank))
                        sem_h, v_hh = h_done[(ICG - 1, tb)]
                        p.wait("tensor", sem_h, v_hh)
                        for j in range(ICG):
                            p.op("tensor", "matmul", psa[:, bank, :],
                                 w2bf[:, buf, j, dcc * 128:(dcc + 1) * 128],
                                 h_sb[:, j, tb * NT:(tb + 1) * NT],
                                 start=(j == 0), stop=(j == ICG - 1),
                                 inc="pe" if j == ICG - 1 else None)
                        v_o = p.val("pe")
                        guards[("w2bf", buf)] = ("pe", v_o)
                        p.wait("vector", "pe", v_o)
                        p.op("vector", "tensor_tensor",
                             result[:, dcc, tb * NT:(tb + 1) * NT],
                             result[:, dcc, tb * NT:(tb + 1) * NT],
                             psa[:, bank, :], ALU.add, inc="dve")
                        guards[("psa", bank)] = ("dve", p.val("dve"))
                guards["h_sb"] = ("pe", p.val("pe"))
                if is_exp:
                    p.op("vector", "scalar_tensor_tensor",
                         result[:, ig, :], wbc[:, e % 2, :],
                         ebias2[:, u * 8 + ig:u * 8 + ig + 1],
                         result[:, ig, :], ALU.mult, ALU.add, inc="dve")
                    if ig == NIG - 1:
                        guards[("wbc", e % 2)] = ("dve", p.val("dve"))
                else:
                    p.op("vector", "tensor_scalar", result[:, ig, :], result[:, ig, :],
                         ebias2[:, u * 8 + ig:u * 8 + ig + 1], None, ALU.add, inc="dve")
            v_ffn = p.val("dve")
            if "ffn" in debug:
                p.wait("gpsimd", "dve", v_ffn)
                p.op("gpsimd", "dma_start", out=dbg_d["res"], in_=result[:], inc="dg")

        if phases >= 3:
            barrier()

            for dcc in range(DC):
                p.op("scalar", "activation", result[:, dcc, :], result[:, dcc, :],
                     AF.Silu, inc="act")
            v_g = p.val("act")
            if "g" in debug:
                p.wait("gpsimd", "act", v_g)
                p.op("gpsimd", "dma_start", out=dbg_d["res"], in_=result[:], inc="dg")

            def stream_qkv(src_ap, slot):
                guard_wait("sync", ("wqkv_st", slot))
                dv = p.op("sync", "dma_start", out=wqkv_st[:, slot, :, :],
                          in_=src_ap.rearrange("(c p) k -> p c k", p=128), inc=f"ds{slot}")
                p.wait("scalar", f"ds{slot}", dv)
                guard_wait("scalar", ("wqkv_bf", slot))
                p.op("scalar", "activation",
                     wqkv_bf[:, slot, :, :].rearrange("p c k -> p (c k)"),
                     wqkv_st[:, slot, :, :].rearrange("p c k -> p (c k)"), AF.Copy, inc="act")
                guards[("wqkv_st", slot)] = ("act", p.val("act"))
                return p.val("act")

            def proj_2048(slot, dst, vready, banks=(0, 1)):
                p.wait("tensor", "act", vready)
                for nb in range(TBN):
                    bank = banks[nb % len(banks)]
                    guard_wait("tensor", ("psa", bank))
                    for c in range(DC):
                        p.op("tensor", "matmul", psa[:, bank, :],
                             wqkv_bf[:, slot, c, :],
                             xn_fm[:, c, nb * NT:(nb + 1) * NT],
                             start=(c == 0), stop=(c == DC - 1),
                             inc="pe" if c == DC - 1 else None)
                    v = p.val("pe")
                    guards[("wqkv_bf", slot)] = ("pe", v)
                    p.wait("vector", "pe", v)
                    p.op("vector", "tensor_copy", dst[:, nb * NT:(nb + 1) * NT],
                         psa[:, bank, :], inc="dve")
                    guards[("psa", bank)] = ("dve", p.val("dve"))
                return p.val("dve")

            v_ksb = proj_2048(0, k_sb, stream_qkv(wk_d, 0))
            v_vsb = proj_2048(1, v_sb, stream_qkv(wv_d, 1))
            p.wait("tensor", "dve", v_vsb)
            for rnd in range(2):
                guard_wait("tensor", "psx")
                for t8 in range(8):
                    mc = rnd * 8 + t8
                    p.op("tensor", "transpose", psx[:, t8 * 128:(t8 + 1) * 128],
                         v_sb[:, mc * 128:(mc + 1) * 128], ident_b[:])
                v = pe_flush()
                p.wait("vector", "pe", v)
                p.op("vector", "tensor_copy", vt_sb[:, rnd * 1024:(rnd + 1) * 1024],
                     psx[:], inc="dve")
                guards["psx"] = ("dve", p.val("dve"))
            v_vt_pe = p.val("pe")
            v_vt = p.val("dve")
            if "kv" in debug:
                p.wait("gpsimd", "dve", v_vt)
                p.op("gpsimd", "dma_start", out=dbg_d["kv"][:, 0, :], in_=k_sb, inc="dg")
                p.op("gpsimd", "dma_start", out=dbg_d["kv"][:, 1, :], in_=vt_sb, inc="dg")

            def proj_q(h):
                slot = h % 2
                vr = stream_qkv(wq_d[h], slot)
                p.wait("tensor", "act", vr)
                for nb in range(TBN):
                    guard_wait("tensor", ("psa", 5))
                    for c in range(DC):
                        p.op("tensor", "matmul", psa[:, 5, :],
                             wqkv_bf[:, slot, c, :],
                             xn_fm[:, c, nb * NT:(nb + 1) * NT],
                             start=(c == 0), stop=(c == DC - 1),
                             inc="pe" if c == DC - 1 else None)
                    v = p.val("pe")
                    guards[("wqkv_bf", slot)] = ("pe", v)
                    p.wait("vector", "pe", v)
                    guard_wait("vector", ("q_sb", h % 2))
                    p.op("vector", "tensor_copy", q_sb[:, h % 2, nb * NT:(nb + 1) * NT],
                         psa[:, 5, :], inc="dve")
                    guards[("psa", 5)] = ("dve", p.val("dve"))
                return p.val("dve")

            guards[("q_sb", 0)] = ("pe", v_vt_pe)
            guards[("psa", "lt", 0)] = guards.get(("psa", 0))
            guards[("psa", "lt", 1)] = guards.get(("psa", 1))
            q_ready = proj_q(0)
            guards[("psa", "lt", 2)] = guards.get(("psa", 5))

            pending = [None]

            def emit_finisher(fin):
                nb, otbank, v_recip = fin
                p.wait("tensor", "dve", v_recip)
                guard_wait("tensor", ("psa", 6))
                p.op("tensor", "matmul", psa[:, 6, :], ones_row[:], sumrec,
                     start=True, stop=True, inc="pe")
                v_rb = p.val("pe")
                guards["sumrec"] = ("pe", v_rb)
                p.wait("vector", "pe", v_rb)
                p.op("vector", "tensor_copy", rb_sb[:, nb % 2, :], psa[:, 6, :], inc="dve")
                guards[("psa", 6)] = ("dve", p.val("dve"))
                p.wait("vector", "dve", p.val("dve"))
                h_, ocat_ap = fin_ctx[nb]
                p.op("vector", "tensor_tensor", ocat_ap,
                     psa[:, otbank, :], rb_sb[:, nb % 2, :], ALU.mult, inc="dve")
                guards[("psa", otbank)] = ("dve", p.val("dve"))

            fin_ctx = {}
            LTB = (0, 1, 5)
            for h in range(NH):
                p.wait("tensor", "dve", q_ready)

                def emit_L(j):
                    nb_, mc_ = divmod(j, 16)
                    guard_wait("tensor", ("psa", "lt", j % 3))
                    p.op("tensor", "matmul", psa[:, LTB[j % 3], :],
                         k_sb[:, mc_ * 128:(mc_ + 1) * 128],
                         q_sb[:, h % 2, nb_ * NT:(nb_ + 1) * NT],
                         start=True, stop=True, inc="pe")
                    return p.val("pe")

                v_l = {0: emit_L(0), 1: emit_L(1)}

                def emit_exp(j):
                    p.wait("scalar", "pe", v_l[j])
                    guard_wait("scalar", ("e_sb", j % 2))
                    p.op("scalar", "activation", e_sb[:, j % 2, :], psa[:, LTB[j % 3], :],
                         AF.Exp, inc="act")
                    v_e = p.val("act")
                    guards[("psa", "lt", j % 3)] = ("act", v_e)
                    return v_e

                def emit_O(j, v_e):
                    nb_, mc_ = divmod(j, 16)
                    otb = 2 + nb_ % 2
                    if mc_ == 0:
                        guard_wait("tensor", ("psa", otb), ("psa", 4))
                    p.wait("tensor", "act", v_e)
                    p.op("tensor", "matmul", psa[:, otb, :],
                         vt_sb[:, mc_ * 128:(mc_ + 1) * 128], e_sb[:, j % 2, :],
                         start=(mc_ == 0), stop=(mc_ == 15))

                def emit_S(j):
                    mc_ = j % 16
                    p.op("tensor", "matmul", psa[0:1, 4, :],
                         ones_col[:], e_sb[:, j % 2, :],
                         start=(mc_ == 0), stop=(mc_ == 15), inc="pe")
                    guards[("e_sb", j % 2)] = ("pe", p.val("pe"))

                for jp in range(0, 64, 2):
                    j0, j1 = jp, jp + 1
                    nb = j0 // 16
                    otbank = 2 + nb % 2
                    if j0 + 2 < 64:
                        v_l[j0 + 2] = emit_L(j0 + 2)
                    if j0 % 16 == 0 and pending[0] is not None:
                        emit_finisher(pending[0])
                        pending[0] = None
                    ve0 = emit_exp(j0)
                    emit_O(j0, ve0)
                    if j1 + 2 < 64:
                        v_l[j1 + 2] = emit_L(j1 + 2)
                    ve1 = emit_exp(j1)
                    emit_O(j1, ve1)
                    emit_S(j0)
                    emit_S(j1)
                    if j1 % 16 == 15:
                        v_osum = p.val("pe")
                        p.wait("vector", "pe", v_osum)
                        guard_wait("vector", "sumrec")
                        p.op("vector", "reciprocal", sumrec, psa[0:1, 4, :], inc="dve")
                        v_recip = p.val("dve")
                        guards[("psa", 4)] = ("dve", v_recip)
                        fin_ctx[nb] = (h, ocat[:, h, nb * NT:(nb + 1) * NT])
                        pending[0] = (nb, otbank, v_recip)
                guards[("q_sb", h % 2)] = ("pe", p.val("pe"))
                if h + 1 < NH:
                    guards[("psa", 5)] = guards.get(("psa", "lt", 2))
                    q_ready = proj_q(h + 1)
                    guards[("psa", "lt", 2)] = guards.get(("psa", 5))
            emit_finisher(pending[0])
            pending[0] = None
            v_attn_pe = p.val("pe")
            v_ocat = p.val("dve")
            if "ocat" in debug:
                p.wait("gpsimd", "dve", v_ocat)
                p.op("gpsimd", "dma_start", out=dbg_d["ocat"],
                     in_=ocat.rearrange("p h n -> p (h n)"), inc="dg")

            wo_slot = [0]
            for hh in range(NH):
                slot = wo_slot[0]; wo_slot[0] ^= 1
                guard_wait("sync", ("wo_st", slot))
                dv = p.op("sync", "dma_start", out=wo_st[:, slot, :], in_=wot_d[hh], inc=f"dc{slot}")
                p.wait("scalar", f"dc{slot}", dv)
                if hh == 0:
                    p.wait("scalar", "pe", v_attn_pe)
                p.op("scalar", "activation", wo_bf[:, hh, :], wo_st[:, slot, :], AF.Copy, inc="act")
                guards[("wo_st", slot)] = ("act", p.val("act"))
            v_wo = p.val("act")

            out_rows = out_d.rearrange("(t p) d -> t p d", p=128)
            xtok_dma = {}
            for ns in range(2):
                xtok_dma[ns] = p.op("sync", "dma_start", out=xtokC[:, ns % 2, :],
                                    in_=x_rows[ns], inc=f"ds{ns % 2}")
            p.wait("tensor", "act", v_wo)
            p.wait("tensor", "dve", v_ocat)
            guards[("psa", 0)] = guards.get(("psa", "lt", 0))
            guards[("psa", 1)] = guards.get(("psa", "lt", 1))
            for ns in range(NTT):
                yb = 2 * (ns % 2)
                gbank = 4 + ns % 2
                guard_wait("tensor", ("psa", yb), ("psa", yb + 1))
                for hh in range(NH):
                    for half in range(2):
                        p.op("tensor", "matmul", psa[:, yb + half, :],
                             ocat[:, hh, ns * 128:(ns + 1) * 128],
                             wo_bf[:, hh, half * 512:(half + 1) * 512],
                             start=(hh == 0), stop=(hh == NH - 1),
                             inc="pe" if (hh == NH - 1 and half == 1) else None)
                v_y = p.val("pe")
                for half in range(2):
                    guard_wait("tensor", ("psa", gbank))
                    for c4 in range(4):
                        c = half * 4 + c4
                        p.op("tensor", "transpose", psa[:, gbank, c4 * 128:(c4 + 1) * 128],
                             result[:, c, ns * 128:(ns + 1) * 128], ident_f[:])
                    v_t = pe_flush()
                    p.wait("vector", "pe", v_t)
                    guard_wait("vector", ("gt", ns % 2))
                    p.op("vector", "tensor_copy",
                         gt_sb[:, ns % 2, half * 512:(half + 1) * 512],
                         psa[:, gbank, :], inc="dve")
                    guards[("psa", gbank)] = ("dve", p.val("dve"))
                p.wait("vector", "pe", v_y)
                p.wait("vector", f"ds{ns % 2}", xtok_dma[ns])
                p.wait("vector", "dve", p.val("dve"))
                yview = psa[:, yb:yb + 2, :].rearrange("p a b -> p (a b)")
                p.op("vector", "tensor_tensor", gt_sb[:, ns % 2, :], yview,
                     gt_sb[:, ns % 2, :], ALU.mult, inc="dve")
                p.wait("vector", "dve", p.val("dve"))
                p.op("vector", "tensor_tensor", xtokC[:, ns % 2, :], xtokC[:, ns % 2, :],
                     gt_sb[:, ns % 2, :], ALU.add, inc="dve")
                v_ep = p.val("dve")
                guards[("psa", yb)] = ("dve", v_ep)
                guards[("psa", yb + 1)] = ("dve", v_ep)
                guards[("gt", ns % 2)] = ("dve", v_ep)
                p.wait("sync", "dve", v_ep)
                p.op("sync", "dma_start", out=out_rows[ns], in_=xtokC[:, ns % 2, :], inc="dc")
                guards[("xtokC", ns % 2)] = ("dc", p.val("dc"))
                if ns + 2 < NTT:
                    guard_wait("sync", ("xtokC", ns % 2))
                    xtok_dma[ns + 2] = p.op("sync", "dma_start", out=xtokC[:, ns % 2, :],
                                            in_=x_rows[ns + 2], inc=f"ds{ns % 2}")
            p.wait("scalar", "dc", p.val("dc"))
            if debug:
                p.wait("gpsimd", "dg", p.val("dg"))
        else:
            if debug:
                p.wait("gpsimd", "dg", p.val("dg"))
            p.wait("gpsimd", "dve", p.val("dve"))
            p.op("gpsimd", "dma_start",
                 out=out_d.rearrange("(c p) d -> p c d", p=128),
                 in_=result[:].rearrange("p c n -> p (c n)").rearrange("p (a b) -> p a b", a=16),
                 inc="dg")
            p.wait("gpsimd", "dg", p.val("dg"))

        p.check()

        with nc.Block() as block:
            @block.tensor
            def _(h):
                for f in p.ops["tensor"]:
                    f(h, sems)

            @block.vector
            def _(h):
                for f in p.ops["vector"]:
                    f(h, sems)

            @block.scalar
            def _(h):
                for f in p.ops["scalar"]:
                    f(h, sems)

            @block.gpsimd
            def _(h):
                for f in p.ops["gpsimd"]:
                    f(h, sems)

            @block.sync
            def _(h):
                for f in p.ops["sync"]:
                    f(h, sems)

    return nc


def make_in_maps(L0_attn_x, L0_attn_proj_q, L0_attn_proj_k, L0_attn_proj_v, L0_attn_proj_o,
                 L0_ffn_gate_w, L0_ffn_gate_b, L0_ffn_sh_w1, L0_ffn_sh_b1, L0_ffn_sh_w2,
                 L0_ffn_sh_b2, L0_ffn_ew1, L0_ffn_eb1, L0_ffn_ew2, L0_ffn_eb2):
    f32 = np.float32
    common = {
        "wq": np.ascontiguousarray(L0_attn_proj_q, f32),
        "wk": np.ascontiguousarray(L0_attn_proj_k, f32),
        "wv": np.ascontiguousarray(L0_attn_proj_v, f32),
        "wot": np.ascontiguousarray(np.asarray(L0_attn_proj_o, f32).transpose(0, 2, 1)),
        "gw": np.ascontiguousarray(L0_ffn_gate_w, f32),
        "gb": np.ascontiguousarray(L0_ffn_gate_b, f32),
        "shw1": np.ascontiguousarray(L0_ffn_sh_w1, f32),
        "shb1": np.ascontiguousarray(L0_ffn_sh_b1, f32),
        "shw2": np.ascontiguousarray(L0_ffn_sh_w2, f32),
        "shb2": np.ascontiguousarray(L0_ffn_sh_b2, f32),
        "ew1": np.ascontiguousarray(L0_ffn_ew1, f32),
        "eb1": np.ascontiguousarray(L0_ffn_eb1, f32),
        "ew2": np.ascontiguousarray(L0_ffn_ew2, f32),
        "eb2": np.ascontiguousarray(L0_ffn_eb2, f32),
    }
    xs = np.ascontiguousarray(L0_attn_x, f32)
    return [dict(common, x=xs[c]) for c in range(8)]


def kernel(**inputs):
    nc = build_nc()
    in_maps = make_in_maps(**inputs)
    res = run_bass_kernel_spmd(nc, in_maps, core_ids=list(range(8)))
    return np.stack([res.results[c]["out"] for c in range(8)], axis=0)


# version 11
# speedup vs baseline: 1.2183x; 1.0252x over previous
import contextlib
import numpy as np
import concourse.bass as bass
import concourse.mybir as mybir
from concourse.bass_utils import run_bass_kernel_spmd

F32 = mybir.dt.float32
BF16 = mybir.dt.bfloat16
AF = mybir.ActivationFunctionType
ALU = mybir.AluOpType
AX = mybir.AxisListType.X

S = 2048
D = 1024
DC = 8
NH = 8
FI = 4096
NE = 8
NU = NE + 1
NTT = 16
NT = 512
TBN = 4
NIG = 8
ICG = 4
EPS_LN = 1e-5
EPS_RT = 1e-9

ENGS = ("tensor", "vector", "scalar", "gpsimd", "sync")
DMA_SEMS = {"ds", "dc", "dg", "ds0", "ds1", "dc0", "dc1", "dgb"}
ALL_SEMS = ("pe", "act", "dve", "pool", "ds", "dc", "dg",
            "ds0", "ds1", "dc0", "dc1", "dgb")


class Plan:

    def __init__(self):
        self.ops = {e: [] for e in ENGS}
        self.cnt = {}
        self.waits = []
        self.last_wait = {}

    def op(self, eng, meth, *args, inc=None, **kwargs):
        step = 16 if inc in DMA_SEMS else 1
        if inc is not None:
            self.cnt[inc] = self.cnt.get(inc, 0) + step
        v = self.cnt.get(inc) if inc is not None else None

        def f(h, sems, meth=meth, args=args, kwargs=kwargs, inc=inc, step=step):
            ins = getattr(h, meth)(*args, **kwargs)
            if inc is not None:
                ins.then_inc(sems[inc], step)
        self.ops[eng].append(f)
        return v

    def wait(self, eng, sem, val):
        if val is None or val <= 0:
            return
        if self.last_wait.get((eng, sem), -1) >= val:
            return
        self.last_wait[(eng, sem)] = val
        self.waits.append((sem, val))

        def f(h, sems, sem=sem, val=val):
            h.wait_ge(sems[sem], val)
        self.ops[eng].append(f)

    def val(self, sem):
        return self.cnt.get(sem, 0)

    def check(self):
        for sem, val in self.waits:
            assert val <= self.cnt.get(sem, 0), \
                f"wait {sem}>={val} never satisfied (final {self.cnt.get(sem, 0)})"


def build_nc(phases=3, debug=()):
    nc = bass.Bass("TRN2", target_bir_lowering=False)
    debug = set(debug)

    ap = lambda t: t.ap()
    x_d = ap(nc.dram_tensor("x", [S, D], F32, kind="ExternalInput"))
    wq_d = ap(nc.dram_tensor("wq", [NH, D, 128], F32, kind="ExternalInput"))
    wk_d = ap(nc.dram_tensor("wk", [D, 128], F32, kind="ExternalInput"))
    wv_d = ap(nc.dram_tensor("wv", [D, 128], F32, kind="ExternalInput"))
    wot_d = ap(nc.dram_tensor("wot", [NH, 128, D], F32, kind="ExternalInput"))
    gw_d = ap(nc.dram_tensor("gw", [D, NE], F32, kind="ExternalInput"))
    gb_d = ap(nc.dram_tensor("gb", [NE], F32, kind="ExternalInput"))
    w1_d = [ap(nc.dram_tensor("shw1", [D, FI], F32, kind="ExternalInput"))]
    b1_d = [ap(nc.dram_tensor("shb1", [FI], F32, kind="ExternalInput"))]
    w2_d = [ap(nc.dram_tensor("shw2", [FI, D], F32, kind="ExternalInput"))]
    b2_d = [ap(nc.dram_tensor("shb2", [D], F32, kind="ExternalInput"))]
    ew1_d = ap(nc.dram_tensor("ew1", [NE, D, FI], F32, kind="ExternalInput"))
    eb1_d = ap(nc.dram_tensor("eb1", [NE, FI], F32, kind="ExternalInput"))
    ew2_d = ap(nc.dram_tensor("ew2", [NE, FI, D], F32, kind="ExternalInput"))
    eb2_d = ap(nc.dram_tensor("eb2", [NE, D], F32, kind="ExternalInput"))
    for e in range(NE):
        w1_d.append(ew1_d[e])
        b1_d.append(eb1_d[e])
        w2_d.append(ew2_d[e])
        b2_d.append(eb2_d[e])
    out_d = ap(nc.dram_tensor("out", [S, D], F32, kind="ExternalOutput"))
    dbg_d = {}
    if debug & {"resA", "ffn", "g"}:
        dbg_d["res"] = ap(nc.dram_tensor("dbg_res", [128, DC, S], F32, kind="ExternalOutput"))
    if "router" in debug:
        dbg_d["wet"] = ap(nc.dram_tensor("dbg_wet", [NE, S], F32, kind="ExternalOutput"))
    if "router2" in debug:
        dbg_d["gbp"] = ap(nc.dram_tensor("dbg_gbp", [8, 1], F32, kind="ExternalOutput"))
        dbg_d["grow"] = ap(nc.dram_tensor("dbg_grow", [1, 8], F32, kind="ExternalOutput"))
        dbg_d["gsb"] = ap(nc.dram_tensor("dbg_gsb", [8, S], F32, kind="ExternalOutput"))
        dbg_d["rtok"] = ap(nc.dram_tensor("dbg_rtok", [128, 128], F32, kind="ExternalOutput"))
        dbg_d["wtok"] = ap(nc.dram_tensor("dbg_wtok", [128, 128], F32, kind="ExternalOutput"))
    if "ocat" in debug:
        dbg_d["ocat"] = ap(nc.dram_tensor("dbg_ocat", [128, NH * S], BF16, kind="ExternalOutput"))
    if "kv" in debug:
        dbg_d["kv"] = ap(nc.dram_tensor("dbg_kv", [128, 2, S], BF16, kind="ExternalOutput"))

    p = Plan()

    with contextlib.ExitStack() as ctx:
        E = ctx.enter_context
        result = E(nc.sbuf_tensor("result", [128, DC, S], F32))
        xn_fm = E(nc.sbuf_tensor("xn_fm", [128, DC, S], BF16))
        abf = E(nc.sbuf_tensor("abf", [128, 27648], BF16))
        af = E(nc.sbuf_tensor("af", [128, 13312], F32))
        ident_f = E(nc.sbuf_tensor("ident_f", [128, 128], F32))
        ident_b = E(nc.sbuf_tensor("ident_b", [128, 128], BF16))
        ones_col = E(nc.sbuf_tensor("ones_col", [128, 1], BF16))
        ones_row = E(nc.sbuf_tensor("ones_row", [1, 128], F32))
        ebias1 = E(nc.sbuf_tensor("ebias1", [128, NU * 32], F32))
        ebias2 = E(nc.sbuf_tensor("ebias2", [128, NU * 8], F32))
        gatebp = E(nc.sbuf_tensor("gatebp", [128, 1], F32))
        gate_st = E(nc.sbuf_tensor("gate_st", [128, 8, 8], F32))
        rts = E(nc.sbuf_tensor("rts", [128, 80], F32))

        bf = abf[:]
        fa = af[:]
        sqN = fa[:, 8192:10240].rearrange("p (b n) -> p b n", b=2)
        xnfN = fa[:, 10240:12288].rearrange("p (b n) -> p b n", b=2)
        statA = fa[:, 12288:12416]
        hi_tokN = bf[:, 16384:18432].rearrange("p (b n) -> p b n", b=2)
        lo_tokN = bf[:, 18432:20480].rearrange("p (b n) -> p b n", b=2)
        lo_seg = [bf[:, o:o + 4096].rearrange("p (c n) -> p c n", c=DC)
                  for o in (4096, 8192, 12288, 20480)]
        wstage = fa[:, 0:4096].rearrange("p (b n) -> p b n", b=2)
        wrowstage = fa[0:1, 0:4096]
        wbc = fa[:, 4096:8192].rearrange("p (b n) -> p b n", b=2)
        g1 = fa[:, 8192:9216].rearrange("p (b n) -> p b n", b=2)
        rtok = fa[:, 8192:8320]
        wtok = fa[:, 8320:8448]
        wrow = fa[0:1, 9216:11264]
        wet = fa[0:8, 11264:13312]
        w1bf = bf[:, 0:8192].rearrange("p (b c i) -> p b c i", b=2, c=DC)
        w2bf = bf[:, 8192:16384].rearrange("p (b j d) -> p b j d", b=2, j=ICG)
        h_sb = bf[:, 16384:24576].rearrange("p (j n) -> p j n", j=ICG)
        ocat = bf[:, 0:16384].rearrange("p (h n) -> p h n", h=NH)
        k_sb = bf[:, 16384:18432]
        vt_sb = bf[:, 18432:20480]
        v_sb = bf[:, 20480:22528]
        q_sb = bf[:, 20480:24576].rearrange("p (b n) -> p b n", b=2)
        e_sb = bf[:, 24576:25600].rearrange("p (b n) -> p b n", b=2)
        wqkv_bf = bf[:, 25600:27648].rearrange("p (b c k) -> p b c k", b=2, c=DC)
        wo_bf = bf[:, 16384:24576].rearrange("p (h d) -> p h d", h=NH)
        rb_sb = fa[:, 0:1024].rearrange("p (b n) -> p b n", b=2)
        gt_sb = fa[:, 1024:3072].rearrange("p (b n) -> p b n", b=2)
        xtokC = fa[:, 3072:5120].rearrange("p (b n) -> p b n", b=2)
        wqkv_st = fa[:, 5120:7168].rearrange("p (b c k) -> p b c k", b=2, c=DC)
        wo_st = fa[:, 7168:9216].rearrange("p (b n) -> p b n", b=2)
        sumrec = fa[0:1, 9216:9728]

        psa = E(nc.psum_tensor("psa", [128, 7, 512], F32))
        psx = E(nc.psum_tensor("psx", [128, 1024], BF16))

        sems = {}
        for s_ in ALL_SEMS:
            sems[s_] = E(nc.semaphore(s_))

        guards = {}

        def guard_wait(eng, *keys):
            for key in keys:
                g = guards.get(key)
                if g:
                    p.wait(eng, g[0], g[1])

        def barrier(skip=()):
            vals = [(s_, p.val(s_)) for s_ in ALL_SEMS if s_ not in skip]
            for e_ in ENGS:
                for s_, v_ in vals:
                    p.wait(e_, s_, v_)

        def pe_flush():
            p.op("tensor", "matmul", psa[0:128, 6, 256:384], ident_b[:],
                 ident_b[:], start=True, stop=True)
            p.op("tensor", "matmul", psa[0:128, 6, 384:512], ident_b[:],
                 ident_b[:], start=True, stop=True, inc="pe")
            return p.val("pe")

        p.op("gpsimd", "memset", ident_f[:], 0.0)
        p.op("gpsimd", "memset", ident_b[:], 0.0, inc="pool")
        p.wait("gpsimd", "pool", p.val("pool"))
        p.op("gpsimd", "affine_select", out=ident_f[:], in_=ident_f[:],
             compare_op=ALU.not_equal, fill=1.0, base=0, pattern=[[-1, 128]], channel_multiplier=1)
        p.op("gpsimd", "affine_select", out=ident_b[:], in_=ident_b[:],
             compare_op=ALU.not_equal, fill=1.0, base=0, pattern=[[-1, 128]], channel_multiplier=1)
        p.op("gpsimd", "memset", ones_col[:], 1.0)
        p.op("gpsimd", "memset", statA[:, 64:65], EPS_LN)
        p.op("gpsimd", "memset", ones_row[:], 1.0, inc="pool")
        v_const = p.val("pool")
        p.wait("tensor", "pool", v_const)
        p.wait("scalar", "pool", v_const)

        p.op("gpsimd", "dma_start", out=gatebp[0:8, :],
             in_=gb_d.rearrange("(p a) -> p a", a=1),
             allow_slow_non_contiguous=True, inc="dgb")
        v_gatebp_dma = p.val("dgb")
        vb1 = {}
        vb2 = {}
        for u in range(NU):
            p.op("gpsimd", "dma_start", out=ebias1[:, u * 32:(u + 1) * 32],
                 in_=b1_d[u].rearrange("(c p) -> p c", p=128),
                 allow_slow_non_contiguous=True, inc="dgb")
            vb1[u] = p.val("dgb")
            p.op("gpsimd", "dma_start", out=ebias2[:, u * 8:(u + 1) * 8],
                 in_=b2_d[u].rearrange("(c p) -> p c", p=128),
                 allow_slow_non_contiguous=True, inc="dgb")
            vb2[u] = p.val("dgb")

        wbc_pool_guard = {}
        wrow_dv = {}
        wbc_ready = {}
        v_wet_h = [None]

        def wbc_dma(e):
            guard_wait("gpsimd", "wrow")
            p.wait("gpsimd", "dve", v_wet_h[0])
            wrow_dv[e] = p.op("gpsimd", "dma_start", out=wrow, in_=wet[e:e + 1, :], inc="dg")

        def build_wbc(e):
            if e not in wrow_dv:
                wbc_dma(e)
            dv = wrow_dv[e]
            p.wait("tensor", "dg", dv)
            pg = wbc_pool_guard.get(e % 2)
            if pg:
                p.wait("vector", "pool", pg)
            for tb in range(TBN):
                guard_wait("tensor", ("psa", 6))
                p.op("tensor", "matmul", psa[:, 6, :], ones_row[:],
                     wrow[:, tb * NT:(tb + 1) * NT], start=True, stop=True, inc="pe")
                v = p.val("pe")
                guards["wrow"] = ("pe", v)
                p.wait("vector", "pe", v)
                guard_wait("vector", ("wbc", e % 2))
                p.op("vector", "tensor_copy", wbc[:, e % 2, tb * NT:(tb + 1) * NT],
                     psa[:, 6, :], inc="dve")
                guards[("psa", 6)] = ("dve", p.val("dve"))
            return p.val("dve")

        wstage_slot = [0]

        def dma_w1(u, ig):
            dvs = []
            for half in range(2):
                slot = wstage_slot[0]; wstage_slot[0] ^= 1
                guard_wait("sync", ("wstage", slot))
                src = w1_d[u].rearrange("(c p) i -> p c i", p=128)[
                    :, :, ig * 512 + half * 256: ig * 512 + (half + 1) * 256]
                dv = p.op("sync", "dma_start",
                          out=wstage[:, slot, :].rearrange("p (c i) -> p c i", c=DC),
                          in_=src, inc=f"ds{slot}")
                dvs.append((slot, dv))
            return dvs

        def conv_w1(ig, dvs):
            buf = ig % 2
            guard_wait("scalar", ("w1bf", buf))
            for half, (slot, dv) in enumerate(dvs):
                p.wait("scalar", f"ds{slot}", dv)
                p.op("scalar", "activation",
                     w1bf[:, buf, :, half * 256:(half + 1) * 256],
                     wstage[:, slot, :].rearrange("p (c i) -> p c i", c=DC), AF.Copy, inc="act")
                guards[("wstage", slot)] = ("act", p.val("act"))
            return p.val("act")

        def dma_w2(u, ig):
            dvs = []
            for half in range(2):
                slot = wstage_slot[0]; wstage_slot[0] ^= 1
                guard_wait("sync", ("wstage", slot))
                src = w2_d[u].rearrange("(g p) d -> p g d", p=128)[
                    :, ig * ICG + half * 2: ig * ICG + (half + 1) * 2, :]
                dv = p.op("sync", "dma_start",
                          out=wstage[:, slot, :].rearrange("p (g d) -> p g d", g=2),
                          in_=src, inc=f"ds{slot}")
                dvs.append((slot, dv))
            return dvs

        def conv_w2(ig, dvs):
            buf = ig % 2
            guard_wait("scalar", ("w2bf", buf))
            for half, (slot, dv) in enumerate(dvs):
                p.wait("scalar", f"ds{slot}", dv)
                p.op("scalar", "activation",
                     w2bf[:, buf, half * 2:(half + 1) * 2, :],
                     wstage[:, slot, :].rearrange("p (j d) -> p j d", j=2), AF.Copy, inc="act")
                guards[("wstage", slot)] = ("act", p.val("act"))
            return p.val("act")

        x_rows = x_d.rearrange("(t p) d -> t p d", p=128)
        rx = result[:].rearrange("p c n -> p (c n)").rearrange("p (t d) -> p t d", t=NTT)
        s1 = statA[:, 0:16]; s2 = statA[:, 16:32]
        rstd = statA[:, 32:48]; nmr = statA[:, 48:64]
        eps_col = statA[:, 64:65]

        def vfence():
            p.wait("vector", "dve", p.val("dve"))

        xload = {}
        for tt in range(NTT):
            eng = "scalar" if tt % 2 == 0 else "sync"
            xload[tt] = p.op(eng, "dma_start", out=rx[:, tt, :],
                             in_=x_rows[tt], inc=f"dc{tt % 2}")
        if phases >= 2:
            d1 = dma_w1(0, 0)
        for tt in range(NTT):
            b = tt % 2
            dv = xload[tt]
            sc = slice(tt, tt + 1)
            p.wait("vector", f"dc{b}", dv)
            p.op("vector", "reduce_sum", s1[:, sc], rx[:, tt, :], axis=AX, inc="dve")
            p.wait("scalar", f"dc{b}", dv)
            guard_wait("scalar", ("sq", b))
            p.op("scalar", "activation", sqN[:, b, :], rx[:, tt, :], AF.Square, inc="act")
            p.wait("vector", "act", p.val("act"))
            p.op("vector", "reduce_sum", s2[:, sc], sqN[:, b, :], axis=AX, inc="dve")
            guards[("sq", b)] = ("dve", p.val("dve"))
        if phases >= 2:
            w1_ready = conv_w1(0, d1)
        vfence()
        p.op("vector", "tensor_scalar", s1, s1, 1.0 / D, None, ALU.mult)
        p.op("vector", "tensor_scalar", s2, s2, 1.0 / D, None, ALU.mult, inc="dve")
        vfence()
        p.op("vector", "tensor_mul", rstd, s1, s1, inc="dve")
        vfence()
        p.op("vector", "tensor_sub", s2, s2, rstd, inc="dve")
        p.wait("scalar", "dve", p.val("dve"))
        p.op("scalar", "activation", s2, s2, AF.Sqrt, bias=eps_col, inc="act")
        p.wait("vector", "act", p.val("act"))
        p.op("vector", "reciprocal", rstd, s2, inc="dve")
        vfence()
        p.op("vector", "scalar_tensor_tensor", nmr, s1, -1.0, rstd,
             ALU.mult, ALU.mult, inc="dve")
        vfence()
        tr_hi = {}
        tr_lo = {}
        for tt in range(NTT):
            b = tt % 2
            sc = slice(tt, tt + 1)
            guard_wait("vector", ("xnf", b))
            p.op("vector", "tensor_scalar", xnfN[:, b, :], rx[:, tt, :],
                 rstd[:, sc], nmr[:, sc], ALU.mult, ALU.add, inc="dve")
            v_apply = p.val("dve")
            p.wait("scalar", "dve", v_apply)
            guard_wait("scalar", ("hi", b), ("hi2", b))
            p.op("scalar", "activation", hi_tokN[:, b, :], xnfN[:, b, :], AF.Copy, inc="act")
            v_hi = p.val("act")
            guard_wait("scalar", ("sq", b))
            p.op("scalar", "activation", sqN[:, b, :], hi_tokN[:, b, :], AF.Copy, inc="act")
            v_hif = p.val("act")
            guards[("hi2", b)] = ("act", v_hif)
            p.wait("vector", "act", v_hif)
            guard_wait("vector", ("lo", b))
            p.op("vector", "tensor_sub", lo_tokN[:, b, :], xnfN[:, b, :], sqN[:, b, :], inc="dve")
            v_lo = p.val("dve")
            guards[("xnf", b)] = ("dve", v_lo)
            guards[("sq", b)] = ("dve", v_lo)
            p.wait("sync", "act", v_hi)
            tr_hi[tt] = p.op("sync", "dma_start", out=xn_fm[:, :, tt * 128:(tt + 1) * 128],
                             in_=hi_tokN[:, b, :].rearrange("p (c n) -> p c n", c=DC),
                             transpose=True, inc="ds0")
            guards[("hi", b)] = ("ds0", tr_hi[tt])
            p.wait("sync", "dve", v_lo)
            tr_lo[tt] = p.op("sync", "dma_start",
                             out=lo_seg[tt // 4][:, :, (tt % 4) * 128:(tt % 4 + 1) * 128],
                             in_=lo_tokN[:, b, :].rearrange("p (c n) -> p c n", c=DC),
                             transpose=True, inc="ds1")
            guards[("lo", b)] = ("ds1", tr_lo[tt])
        if phases >= 2:
            d2 = dma_w2(0, 0)
        v_comb = {}
        for s in range(4):
            p.wait("vector", "ds0", tr_hi[4 * s + 3])
            p.wait("vector", "ds1", tr_lo[4 * s + 3])
            p.op("vector", "tensor_tensor", result[:, :, s * 512:(s + 1) * 512],
                 xn_fm[:, :, s * 512:(s + 1) * 512], lo_seg[s], ALU.add, inc="dve")
            v_comb[s] = p.val("dve")
        guards[("w1bf", 1)] = ("dve", v_comb[0])
        guards[("w2bf", 0)] = ("dve", v_comb[1])
        guards[("w2bf", 1)] = ("dve", v_comb[2])
        barrier(skip=("dgb", "ds0", "ds1"))

        if "resA" in debug:
            p.op("gpsimd", "dma_start", out=dbg_d["res"], in_=result[:], inc="dg")

        dvgw = p.op("sync", "dma_start", out=gate_st[:],
                    in_=gw_d.rearrange("(c p) e -> p c e", p=128), inc="dc")
        gate_sb = fa[0:8, 9216:11264]
        def router_p1():
            p.wait("tensor", "dc", dvgw)
            p.wait("vector", "dgb", v_gatebp_dma)
            for tb in range(TBN):
                bank = tb % 2
                guard_wait("tensor", ("psa", bank))
                for c in range(DC):
                    p.op("tensor", "matmul", psa[0:8, bank, :], gate_st[:, c, :],
                         result[:, c, tb * NT:(tb + 1) * NT],
                         start=(c == 0), stop=(c == DC - 1),
                         inc="pe" if c == DC - 1 else None)
                v = p.val("pe")
                p.wait("vector", "pe", v)
                guard_wait("vector", "wrowstage")
                p.op("vector", "tensor_scalar", gate_sb[:, tb * NT:(tb + 1) * NT],
                     psa[0:8, bank, :], gatebp[0:8, :], None, ALU.add, inc="dve")
                guards[("psa", bank)] = ("dve", p.val("dve"))

        def router_p2():
            v_gate = p.val("dve")
            if "router2" in debug:
                p.wait("gpsimd", "dve", v_gate)
                p.op("gpsimd", "dma_start", out=dbg_d["gbp"], in_=gatebp[0:8, :], inc="dg")
                p.op("gpsimd", "dma_start", out=dbg_d["gsb"], in_=gate_sb, inc="dg")
            p.wait("tensor", "dve", v_gate)
            guard_wait("tensor", ("psa", 2))
            for tt in range(NTT):
                p.op("tensor", "transpose", psa[0:128, 2, tt * 8:(tt + 1) * 8],
                     gate_sb[:, tt * 128:(tt + 1) * 128], ident_f[0:8, 0:8])
            v = pe_flush()
            p.wait("vector", "pe", v)
            p.op("vector", "tensor_copy", rtok, psa[:, 2, 0:128], inc="dve")
            guards[("psa", 2)] = ("dve", p.val("dve"))
            r3 = rtok.rearrange("p (a e) -> p a e", e=8)
            w3 = wtok.rearrange("p (a e) -> p a e", e=8)
            mx = rts[:, 0:16]; sm = rts[:, 16:32]
            m1 = rts[:, 32:48]; m2 = rts[:, 48:64]; dn = rts[:, 64:80]

            def bb(col):
                return col.rearrange("p (a o) -> p a o", o=1).broadcast_to([128, NTT, 8])

            def vf():
                p.wait("vector", "dve", p.val("dve"))
            vf()
            p.op("vector", "reduce_max", mx, r3, axis=AX, negate=True, inc="dve")
            vf()
            p.op("vector", "tensor_tensor", w3, r3, bb(mx), ALU.add, inc="dve")
            p.wait("scalar", "dve", p.val("dve"))
            p.op("scalar", "activation", wtok, wtok, AF.Exp, inc="act")
            p.wait("vector", "act", p.val("act"))
            p.op("vector", "reduce_sum", sm, w3, axis=AX, inc="dve")
            vf()
            p.op("vector", "reciprocal", sm, sm, inc="dve")
            vf()
            p.op("vector", "tensor_tensor", r3, w3, bb(sm), ALU.mult, inc="dve")
            vf()
            p.op("vector", "reduce_max", m1, r3, axis=AX, inc="dve")
            vf()
            p.op("vector", "tensor_tensor", w3, r3, bb(m1), ALU.is_equal, inc="dve")
            vf()
            p.op("vector", "scalar_tensor_tensor", w3, w3, -2.0, r3, ALU.mult, ALU.add, inc="dve")
            vf()
            p.op("vector", "reduce_max", m2, w3, axis=AX, inc="dve")
            vf()
            p.op("vector", "tensor_add", dn, m1, m2, inc="dve")
            p.op("vector", "tensor_tensor", w3, r3, bb(m2), ALU.is_ge, inc="dve")
            vf()
            p.op("vector", "tensor_scalar", dn, dn, EPS_RT, None, ALU.add, inc="dve")
            p.op("vector", "tensor_mul", w3, w3, r3, inc="dve")
            vf()
            p.op("vector", "reciprocal", dn, dn, inc="dve")
            vf()
            p.op("vector", "tensor_tensor", w3, w3, bb(dn), ALU.mult, inc="dve")

        def router_p3():
            v_wtok = p.val("dve")
            if "router2" in debug:
                p.wait("gpsimd", "dve", v_wtok)
                p.op("gpsimd", "dma_start", out=dbg_d["rtok"], in_=rtok, inc="dg")
                p.op("gpsimd", "dma_start", out=dbg_d["wtok"], in_=wtok, inc="dg")
            p.wait("tensor", "dve", v_wtok)
            for g in range(4):
                bank = 2 + g % 2
                guard_wait("tensor", ("psa", bank))
                for j in range(4):
                    tt = 4 * g + j
                    p.op("tensor", "transpose", psa[0:8, bank, j * 128:(j + 1) * 128],
                         wtok[:, tt * 8:(tt + 1) * 8], ident_f[:])
                v = pe_flush()
                p.wait("vector", "pe", v)
                p.op("vector", "tensor_copy", wet[:, g * 512:(g + 1) * 512],
                     psa[0:8, bank, :], inc="dve")
                guards[("psa", bank)] = ("dve", p.val("dve"))
            v_wet = p.val("dve")
            if "router" in debug:
                p.wait("gpsimd", "dve", v_wet)
                p.op("gpsimd", "dma_start", out=dbg_d["wet"], in_=wet, inc="dg")
            return v_wet

        if phases >= 2:
            gelu_j = [0]
            units = [(u, ig) for u in range(NU) for ig in range(NIG)]
            for idx, (u, ig) in enumerate(units):
                is_exp = u >= 1
                e = u - 1
                buf = ig % 2
                if ig == 0:
                    p.wait("scalar", "dgb", vb1[u])
                    p.wait("vector", "dgb", vb2[u])
                if is_exp and ig == 0:
                    p.wait("gpsimd", "dve", wbc_ready[e])
                w2_ready = conv_w2(ig, d2)
                nxt = units[idx + 1] if idx + 1 < len(units) else None
                if nxt:
                    d1 = dma_w1(*nxt)
                p.wait("tensor", "act", w1_ready)
                h_done = {}
                for ic in range(ICG):
                    icg = ig * ICG + ic
                    for tb in range(TBN):
                        bank = (ic * TBN + tb) % 3
                        guard_wait("tensor", ("psa", bank))
                        for c in range(DC):
                            p.op("tensor", "matmul", psa[:, bank, :],
                                 w1bf[:, buf, c, ic * 128:(ic + 1) * 128],
                                 xn_fm[:, c, tb * NT:(tb + 1) * NT],
                                 start=(c == 0), stop=(c == DC - 1),
                                 inc="pe" if c == DC - 1 else None)
                        v_h = p.val("pe")
                        guards[("w1bf", buf)] = ("pe", v_h)
                        p.wait("scalar", "pe", v_h)
                        bias_ap = ebias1[:, u * 32 + icg:u * 32 + icg + 1]
                        if is_exp:
                            j = gelu_j[0] % 2
                            gelu_j[0] += 1
                            guard_wait("scalar", ("g1", j))
                            p.op("scalar", "activation", g1[:, j, :], psa[:, bank, :],
                                 AF.Gelu_apprx_tanh, bias=bias_ap, inc="act")
                            v_gl = p.val("act")
                            guards[("psa", bank)] = ("act", v_gl)
                            p.wait("gpsimd", "act", v_gl)
                            guard_wait("gpsimd", "h_sb")
                            p.op("gpsimd", "tensor_mul", h_sb[:, ic, tb * NT:(tb + 1) * NT],
                                 g1[:, j, :], wbc[:, e % 2, tb * NT:(tb + 1) * NT], inc="pool")
                            guards[("g1", j)] = ("pool", p.val("pool"))
                            h_done[(ic, tb)] = ("pool", p.val("pool"))
                        else:
                            guard_wait("scalar", "h_sb")
                            p.op("scalar", "activation", h_sb[:, ic, tb * NT:(tb + 1) * NT],
                                 psa[:, bank, :], AF.Gelu_apprx_tanh, bias=bias_ap, inc="act")
                            guards[("psa", bank)] = ("act", p.val("act"))
                            h_done[(ic, tb)] = ("act", p.val("act"))
                if is_exp:
                    wbc_pool_guard[e % 2] = p.val("pool")
                if u == 0:
                    if ig == 0:
                        router_p1()
                    elif ig == 1:
                        router_p2()
                    elif ig == 2:
                        v_wet_h[0] = router_p3()
                if nxt:
                    w1_ready = conv_w1(nxt[1], d1)
                    d2 = dma_w2(*nxt)
                if u + 1 < NU and ig == (2 if u == 0 else 6):
                    pass
                if u + 1 < NU and ig == (3 if u == 0 else 0):
                    wbc_ready[u] = build_wbc(u)
                if u + 1 < NU and ig == (2 if u == 0 else 5):
                    wbc_dma(u)
                p.wait("tensor", "act", w2_ready)
                for dcc in range(DC):
                    for tb in range(TBN):
                        bank = 3 + (dcc * TBN + tb) % 3
                        guard_wait("tensor", ("psa", bank))
                        sem_h, v_hh = h_done[(ICG - 1, tb)]
                        p.wait("tensor", sem_h, v_hh)
                        for j in range(ICG):
                            p.op("tensor", "matmul", psa[:, bank, :],
                                 w2bf[:, buf, j, dcc * 128:(dcc + 1) * 128],
                                 h_sb[:, j, tb * NT:(tb + 1) * NT],
                                 start=(j == 0), stop=(j == ICG - 1),
                                 inc="pe" if j == ICG - 1 else None)
                        v_o = p.val("pe")
                        guards[("w2bf", buf)] = ("pe", v_o)
                        p.wait("vector", "pe", v_o)
                        p.op("vector", "tensor_tensor",
                             result[:, dcc, tb * NT:(tb + 1) * NT],
                             result[:, dcc, tb * NT:(tb + 1) * NT],
                             psa[:, bank, :], ALU.add, inc="dve")
                        guards[("psa", bank)] = ("dve", p.val("dve"))
                guards["h_sb"] = ("pe", p.val("pe"))
                if is_exp:
                    p.op("vector", "scalar_tensor_tensor",
                         result[:, ig, :], wbc[:, e % 2, :],
                         ebias2[:, u * 8 + ig:u * 8 + ig + 1],
                         result[:, ig, :], ALU.mult, ALU.add, inc="dve")
                    if ig == NIG - 1:
                        guards[("wbc", e % 2)] = ("dve", p.val("dve"))
                else:
                    p.op("vector", "tensor_scalar", result[:, ig, :], result[:, ig, :],
                         ebias2[:, u * 8 + ig:u * 8 + ig + 1], None, ALU.add, inc="dve")
            v_ffn = p.val("dve")
            if "ffn" in debug:
                p.wait("gpsimd", "dve", v_ffn)
                p.op("gpsimd", "dma_start", out=dbg_d["res"], in_=result[:], inc="dg")

        if phases >= 3:
            barrier()

            for dcc in range(DC):
                p.op("scalar", "activation", result[:, dcc, :], result[:, dcc, :],
                     AF.Silu, inc="act")
            v_g = p.val("act")
            if "g" in debug:
                p.wait("gpsimd", "act", v_g)
                p.op("gpsimd", "dma_start", out=dbg_d["res"], in_=result[:], inc="dg")

            def stream_qkv(src_ap, slot):
                guard_wait("sync", ("wqkv_st", slot))
                dv = p.op("sync", "dma_start", out=wqkv_st[:, slot, :, :],
                          in_=src_ap.rearrange("(c p) k -> p c k", p=128), inc=f"ds{slot}")
                p.wait("scalar", f"ds{slot}", dv)
                guard_wait("scalar", ("wqkv_bf", slot))
                p.op("scalar", "activation",
                     wqkv_bf[:, slot, :, :].rearrange("p c k -> p (c k)"),
                     wqkv_st[:, slot, :, :].rearrange("p c k -> p (c k)"), AF.Copy, inc="act")
                guards[("wqkv_st", slot)] = ("act", p.val("act"))
                return p.val("act")

            def proj_2048(slot, dst, vready, banks=(0, 1)):
                p.wait("tensor", "act", vready)
                for nb in range(TBN):
                    bank = banks[nb % len(banks)]
                    guard_wait("tensor", ("psa", bank))
                    for c in range(DC):
                        p.op("tensor", "matmul", psa[:, bank, :],
                             wqkv_bf[:, slot, c, :],
                             xn_fm[:, c, nb * NT:(nb + 1) * NT],
                             start=(c == 0), stop=(c == DC - 1),
                             inc="pe" if c == DC - 1 else None)
                    v = p.val("pe")
                    guards[("wqkv_bf", slot)] = ("pe", v)
                    p.wait("vector", "pe", v)
                    p.op("vector", "tensor_copy", dst[:, nb * NT:(nb + 1) * NT],
                         psa[:, bank, :], inc="dve")
                    guards[("psa", bank)] = ("dve", p.val("dve"))
                return p.val("dve")

            v_ksb = proj_2048(0, k_sb, stream_qkv(wk_d, 0))
            v_vsb = proj_2048(1, v_sb, stream_qkv(wv_d, 1))
            p.wait("tensor", "dve", v_vsb)
            for rnd in range(2):
                guard_wait("tensor", "psx")
                for t8 in range(8):
                    mc = rnd * 8 + t8
                    p.op("tensor", "transpose", psx[:, t8 * 128:(t8 + 1) * 128],
                         v_sb[:, mc * 128:(mc + 1) * 128], ident_b[:])
                v = pe_flush()
                p.wait("vector", "pe", v)
                p.op("vector", "tensor_copy", vt_sb[:, rnd * 1024:(rnd + 1) * 1024],
                     psx[:], inc="dve")
                guards["psx"] = ("dve", p.val("dve"))
            v_vt_pe = p.val("pe")
            v_vt = p.val("dve")
            if "kv" in debug:
                p.wait("gpsimd", "dve", v_vt)
                p.op("gpsimd", "dma_start", out=dbg_d["kv"][:, 0, :], in_=k_sb, inc="dg")
                p.op("gpsimd", "dma_start", out=dbg_d["kv"][:, 1, :], in_=vt_sb, inc="dg")

            def proj_q(h):
                slot = h % 2
                vr = stream_qkv(wq_d[h], slot)
                p.wait("tensor", "act", vr)
                for nb in range(TBN):
                    guard_wait("tensor", ("psa", 5))
                    for c in range(DC):
                        p.op("tensor", "matmul", psa[:, 5, :],
                             wqkv_bf[:, slot, c, :],
                             xn_fm[:, c, nb * NT:(nb + 1) * NT],
                             start=(c == 0), stop=(c == DC - 1),
                             inc="pe" if c == DC - 1 else None)
                    v = p.val("pe")
                    guards[("wqkv_bf", slot)] = ("pe", v)
                    p.wait("vector", "pe", v)
                    guard_wait("vector", ("q_sb", h % 2))
                    p.op("vector", "tensor_copy", q_sb[:, h % 2, nb * NT:(nb + 1) * NT],
                         psa[:, 5, :], inc="dve")
                    guards[("psa", 5)] = ("dve", p.val("dve"))
                return p.val("dve")

            guards[("q_sb", 0)] = ("pe", v_vt_pe)
            guards[("psa", "lt", 0)] = guards.get(("psa", 0))
            guards[("psa", "lt", 1)] = guards.get(("psa", 1))
            q_ready = proj_q(0)
            guards[("psa", "lt", 2)] = guards.get(("psa", 5))

            pending = [None]

            def emit_finisher(fin):
                nb, otbank, v_recip = fin
                p.wait("tensor", "dve", v_recip)
                guard_wait("tensor", ("psa", 6))
                p.op("tensor", "matmul", psa[:, 6, :], ones_row[:], sumrec,
                     start=True, stop=True, inc="pe")
                v_rb = p.val("pe")
                guards["sumrec"] = ("pe", v_rb)
                p.wait("vector", "pe", v_rb)
                p.op("vector", "tensor_copy", rb_sb[:, nb % 2, :], psa[:, 6, :], inc="dve")
                guards[("psa", 6)] = ("dve", p.val("dve"))
                p.wait("vector", "dve", p.val("dve"))
                h_, ocat_ap = fin_ctx[nb]
                p.op("vector", "tensor_tensor", ocat_ap,
                     psa[:, otbank, :], rb_sb[:, nb % 2, :], ALU.mult, inc="dve")
                guards[("psa", otbank)] = ("dve", p.val("dve"))

            fin_ctx = {}
            LTB = (0, 1, 5)
            for h in range(NH):
                p.wait("tensor", "dve", q_ready)

                def emit_L(j):
                    nb_, mc_ = divmod(j, 16)
                    guard_wait("tensor", ("psa", "lt", j % 3))
                    p.op("tensor", "matmul", psa[:, LTB[j % 3], :],
                         k_sb[:, mc_ * 128:(mc_ + 1) * 128],
                         q_sb[:, h % 2, nb_ * NT:(nb_ + 1) * NT],
                         start=True, stop=True, inc="pe")
                    return p.val("pe")

                v_l = {0: emit_L(0), 1: emit_L(1)}

                def emit_exp(j):
                    p.wait("scalar", "pe", v_l[j])
                    guard_wait("scalar", ("e_sb", j % 2))
                    p.op("scalar", "activation", e_sb[:, j % 2, :], psa[:, LTB[j % 3], :],
                         AF.Exp, inc="act")
                    v_e = p.val("act")
                    guards[("psa", "lt", j % 3)] = ("act", v_e)
                    return v_e

                def emit_O(j, v_e):
                    nb_, mc_ = divmod(j, 16)
                    otb = 2 + nb_ % 2
                    if mc_ == 0:
                        guard_wait("tensor", ("psa", otb), ("psa", 4))
                    p.wait("tensor", "act", v_e)
                    p.op("tensor", "matmul", psa[:, otb, :],
                         vt_sb[:, mc_ * 128:(mc_ + 1) * 128], e_sb[:, j % 2, :],
                         start=(mc_ == 0), stop=(mc_ == 15))

                def emit_S(j):
                    mc_ = j % 16
                    p.op("tensor", "matmul", psa[0:1, 4, :],
                         ones_col[:], e_sb[:, j % 2, :],
                         start=(mc_ == 0), stop=(mc_ == 15), inc="pe")
                    guards[("e_sb", j % 2)] = ("pe", p.val("pe"))

                for jp in range(0, 64, 2):
                    j0, j1 = jp, jp + 1
                    nb = j0 // 16
                    otbank = 2 + nb % 2
                    if j0 + 2 < 64:
                        v_l[j0 + 2] = emit_L(j0 + 2)
                    if j0 % 16 == 0 and pending[0] is not None:
                        emit_finisher(pending[0])
                        pending[0] = None
                    ve0 = emit_exp(j0)
                    emit_O(j0, ve0)
                    if j1 + 2 < 64:
                        v_l[j1 + 2] = emit_L(j1 + 2)
                    ve1 = emit_exp(j1)
                    emit_O(j1, ve1)
                    emit_S(j0)
                    emit_S(j1)
                    if j1 % 16 == 15:
                        v_osum = p.val("pe")
                        p.wait("vector", "pe", v_osum)
                        guard_wait("vector", "sumrec")
                        p.op("vector", "reciprocal", sumrec, psa[0:1, 4, :], inc="dve")
                        v_recip = p.val("dve")
                        guards[("psa", 4)] = ("dve", v_recip)
                        fin_ctx[nb] = (h, ocat[:, h, nb * NT:(nb + 1) * NT])
                        pending[0] = (nb, otbank, v_recip)
                guards[("q_sb", h % 2)] = ("pe", p.val("pe"))
                if h + 1 < NH:
                    guards[("psa", 5)] = guards.get(("psa", "lt", 2))
                    q_ready = proj_q(h + 1)
                    guards[("psa", "lt", 2)] = guards.get(("psa", 5))
            emit_finisher(pending[0])
            pending[0] = None
            v_attn_pe = p.val("pe")
            v_ocat = p.val("dve")
            if "ocat" in debug:
                p.wait("gpsimd", "dve", v_ocat)
                p.op("gpsimd", "dma_start", out=dbg_d["ocat"],
                     in_=ocat.rearrange("p h n -> p (h n)"), inc="dg")

            wo_slot = [0]
            for hh in range(NH):
                slot = wo_slot[0]; wo_slot[0] ^= 1
                guard_wait("sync", ("wo_st", slot))
                dv = p.op("sync", "dma_start", out=wo_st[:, slot, :], in_=wot_d[hh], inc=f"dc{slot}")
                p.wait("scalar", f"dc{slot}", dv)
                if hh == 0:
                    p.wait("scalar", "pe", v_attn_pe)
                p.op("scalar", "activation", wo_bf[:, hh, :], wo_st[:, slot, :], AF.Copy, inc="act")
                guards[("wo_st", slot)] = ("act", p.val("act"))
            v_wo = p.val("act")

            out_rows = out_d.rearrange("(t p) d -> t p d", p=128)
            xtok_dma = {}
            for ns in range(2):
                xtok_dma[ns] = p.op("sync", "dma_start", out=xtokC[:, ns % 2, :],
                                    in_=x_rows[ns], inc=f"ds{ns % 2}")
            p.wait("tensor", "act", v_wo)
            p.wait("tensor", "dve", v_ocat)
            guards[("psa", 0)] = guards.get(("psa", "lt", 0))
            guards[("psa", 1)] = guards.get(("psa", "lt", 1))
            for ns in range(NTT):
                yb = 2 * (ns % 2)
                gbank = 4 + ns % 2
                guard_wait("tensor", ("psa", yb), ("psa", yb + 1))
                for hh in range(NH):
                    for half in range(2):
                        p.op("tensor", "matmul", psa[:, yb + half, :],
                             ocat[:, hh, ns * 128:(ns + 1) * 128],
                             wo_bf[:, hh, half * 512:(half + 1) * 512],
                             start=(hh == 0), stop=(hh == NH - 1),
                             inc="pe" if (hh == NH - 1 and half == 1) else None)
                v_y = p.val("pe")
                for half in range(2):
                    guard_wait("tensor", ("psa", gbank))
                    for c4 in range(4):
                        c = half * 4 + c4
                        p.op("tensor", "transpose", psa[:, gbank, c4 * 128:(c4 + 1) * 128],
                             result[:, c, ns * 128:(ns + 1) * 128], ident_f[:])
                    v_t = pe_flush()
                    p.wait("vector", "pe", v_t)
                    guard_wait("vector", ("gt", ns % 2))
                    p.op("vector", "tensor_copy",
                         gt_sb[:, ns % 2, half * 512:(half + 1) * 512],
                         psa[:, gbank, :], inc="dve")
                    guards[("psa", gbank)] = ("dve", p.val("dve"))
                p.wait("vector", "pe", v_y)
                p.wait("vector", f"ds{ns % 2}", xtok_dma[ns])
                p.wait("vector", "dve", p.val("dve"))
                yview = psa[:, yb:yb + 2, :].rearrange("p a b -> p (a b)")
                p.op("vector", "tensor_tensor", gt_sb[:, ns % 2, :], yview,
                     gt_sb[:, ns % 2, :], ALU.mult, inc="dve")
                p.wait("vector", "dve", p.val("dve"))
                p.op("vector", "tensor_tensor", xtokC[:, ns % 2, :], xtokC[:, ns % 2, :],
                     gt_sb[:, ns % 2, :], ALU.add, inc="dve")
                v_ep = p.val("dve")
                guards[("psa", yb)] = ("dve", v_ep)
                guards[("psa", yb + 1)] = ("dve", v_ep)
                guards[("gt", ns % 2)] = ("dve", v_ep)
                p.wait("sync", "dve", v_ep)
                p.op("sync", "dma_start", out=out_rows[ns], in_=xtokC[:, ns % 2, :], inc="dc")
                guards[("xtokC", ns % 2)] = ("dc", p.val("dc"))
                if ns + 2 < NTT:
                    guard_wait("sync", ("xtokC", ns % 2))
                    xtok_dma[ns + 2] = p.op("sync", "dma_start", out=xtokC[:, ns % 2, :],
                                            in_=x_rows[ns + 2], inc=f"ds{ns % 2}")
            p.wait("scalar", "dc", p.val("dc"))
            if debug:
                p.wait("gpsimd", "dg", p.val("dg"))
        else:
            if debug:
                p.wait("gpsimd", "dg", p.val("dg"))
            p.wait("gpsimd", "dve", p.val("dve"))
            p.op("gpsimd", "dma_start",
                 out=out_d.rearrange("(c p) d -> p c d", p=128),
                 in_=result[:].rearrange("p c n -> p (c n)").rearrange("p (a b) -> p a b", a=16),
                 inc="dg")
            p.wait("gpsimd", "dg", p.val("dg"))

        p.check()

        with nc.Block() as block:
            @block.tensor
            def _(h):
                for f in p.ops["tensor"]:
                    f(h, sems)

            @block.vector
            def _(h):
                for f in p.ops["vector"]:
                    f(h, sems)

            @block.scalar
            def _(h):
                for f in p.ops["scalar"]:
                    f(h, sems)

            @block.gpsimd
            def _(h):
                for f in p.ops["gpsimd"]:
                    f(h, sems)

            @block.sync
            def _(h):
                for f in p.ops["sync"]:
                    f(h, sems)

    return nc


def make_in_maps(L0_attn_x, L0_attn_proj_q, L0_attn_proj_k, L0_attn_proj_v, L0_attn_proj_o,
                 L0_ffn_gate_w, L0_ffn_gate_b, L0_ffn_sh_w1, L0_ffn_sh_b1, L0_ffn_sh_w2,
                 L0_ffn_sh_b2, L0_ffn_ew1, L0_ffn_eb1, L0_ffn_ew2, L0_ffn_eb2):
    f32 = np.float32
    common = {
        "wq": np.ascontiguousarray(L0_attn_proj_q, f32),
        "wk": np.ascontiguousarray(L0_attn_proj_k, f32),
        "wv": np.ascontiguousarray(L0_attn_proj_v, f32),
        "wot": np.ascontiguousarray(np.asarray(L0_attn_proj_o, f32).transpose(0, 2, 1)),
        "gw": np.ascontiguousarray(L0_ffn_gate_w, f32),
        "gb": np.ascontiguousarray(L0_ffn_gate_b, f32),
        "shw1": np.ascontiguousarray(L0_ffn_sh_w1, f32),
        "shb1": np.ascontiguousarray(L0_ffn_sh_b1, f32),
        "shw2": np.ascontiguousarray(L0_ffn_sh_w2, f32),
        "shb2": np.ascontiguousarray(L0_ffn_sh_b2, f32),
        "ew1": np.ascontiguousarray(L0_ffn_ew1, f32),
        "eb1": np.ascontiguousarray(L0_ffn_eb1, f32),
        "ew2": np.ascontiguousarray(L0_ffn_ew2, f32),
        "eb2": np.ascontiguousarray(L0_ffn_eb2, f32),
    }
    xs = np.ascontiguousarray(L0_attn_x, f32)
    return [dict(common, x=xs[c]) for c in range(8)]


def kernel(**inputs):
    nc = build_nc()
    in_maps = make_in_maps(**inputs)
    res = run_bass_kernel_spmd(nc, in_maps, core_ids=list(range(8)))
    return np.stack([res.results[c]["out"] for c in range(8)], axis=0)
